# version 32
# baseline (speedup 1.0000x reference)
import numpy as np
import concourse.bass as bass
import concourse.mybir as mybir
from concourse.bass_utils import run_bass_kernel_spmd
from contextlib import ExitStack

F32 = mybir.dt.float32
BF16 = mybir.dt.bfloat16
AF = mybir.ActivationFunctionType
ALU = mybir.AluOpType
AX = mybir.AxisListType

C = 1024
L = 256
NC = C // 128
NL = L // 128
STW = 512
NJ = STW // 128
EPS = 1e-5
SCALE = 1.0 / 32.0

ENGINES = ("sync", "gpsimd", "vector", "scalar", "tensor")


class Rec:

    def __init__(self):
        self.ops = {e: [] for e in ENGINES}
        self.meta = {e: [] for e in ENGINES}
        self.cnt = {}
        self.label = ""

    def wait(self, eng, sem, val):
        if val <= 0:
            return
        name = None
        for n, s in self.sems.items():
            if s is sem:
                name = n
                break
        self.ops[eng].append(lambda e, s=sem, v=val: e.wait_ge(s, v))
        self.meta[eng].append(("wait", name, val, self.label))

    OWN = {"vector": "dve", "scalar": "act", "gpsimd": "gp"}

    def emit(self, eng, fn, sem=None, k=1, dma=False, selfdep=True):
        if not dma and eng in self.OWN:
            own = self.OWN[eng]
            assert sem is None or sem == own, (eng, sem)
            pre = self.cnt.get(own, 0)
            if selfdep and pre > 0:
                self.ops[eng].append(
                    lambda e, s=self.sems[own], v=pre: e.wait_ge(s, v))
                self.meta[eng].append(("wait", own, pre, self.label))
            sem, k = own, 1
        if sem is None or k is None:
            self.ops[eng].append(fn)
            self.meta[eng].append(("op", None, 0, self.label))
            return None
        name = sem
        self.cnt[name] = self.cnt.get(name, 0) + k

        def wrapped(e, fn=fn, k=k):
            fn(e).then_inc(self.sems[name], k)

        self.ops[eng].append(wrapped)
        self.meta[eng].append(("op", name, k, self.label))
        return self.cnt[name]


def build_program(TC, dbg=False):
    NST = TC // STW
    NG = TC // 128

    nc = bass.Bass("TRN2", target_bir_lowering=False, debug=False)
    if dbg:
        nc.dbg_aps_list = []

    x_d = nc.dram_tensor("x", [TC + 1, C], F32, kind="ExternalInput").ap()
    wtt_d = nc.dram_tensor("wtt", [C, L], F32, kind="ExternalInput").ap()
    wpt_d = nc.dram_tensor("wpt", [C, L], F32, kind="ExternalInput").ap()
    wgt_d = nc.dram_tensor("wgt", [C, C], F32, kind="ExternalInput").ap()
    wrt_d = nc.dram_tensor("wrt", [C, C], F32, kind="ExternalInput").ap()
    wkt_d = nc.dram_tensor("wkt", [C, C], F32, kind="ExternalInput").ap()
    wvt_d = nc.dram_tensor("wvt", [C, C], F32, kind="ExternalInput").ap()
    mur_d = nc.dram_tensor("mur", [128, NC], F32, kind="ExternalInput").ap()
    muk_d = nc.dram_tensor("muk", [128, NC], F32, kind="ExternalInput").ap()
    out_d = nc.dram_tensor("out", [TC, C], F32, kind="ExternalOutput").ap()

    gate_d = nc.dram_tensor("gate_hbm", [TC, C], BF16).ap()
    if dbg:
        dbg_st = nc.dram_tensor("dbg_st", [128, 48], F32, kind="ExternalOutput").ap()
        dbg_th = nc.dram_tensor("dbg_th", [128, 2 * TC], F32, kind="ExternalOutput").ap()
        dbg_pac = nc.dram_tensor("dbg_pac", [128, 2048], F32, kind="ExternalOutput").ap()
        dbg_gate = nc.dram_tensor("dbg_gate", [TC, C], F32, kind="ExternalOutput").ap()
        dbg_xnf = nc.dram_tensor("dbg_xnf", [128, NC * STW], F32, kind="ExternalOutput").ap()
    else:
        dbg_st = dbg_th = dbg_pac = dbg_gate = dbg_xnf = None
    pp_in = nc.dram_tensor("pp_in", [L, C], BF16)
    pp_out = nc.dram_tensor("pp_out", [2 * L, C], BF16)
    ppA_in = nc.dram_tensor("ppA_in", [L, C], F32)
    ppA_out = nc.dram_tensor("ppA_out", [2 * L, C], F32)

    r = Rec()
    es = ExitStack()
    sb = lambda name, shape, dt: es.enter_context(nc.sbuf_tensor(name, shape, dt))
    ps = lambda name, shape, dt: es.enter_context(nc.psum_tensor(name, shape, dt))

    w_th = sb("w_th", [128, NC * L], BF16)
    w_ph = sb("w_ph", [128, NC * L], BF16)
    w_g = sb("w_g", [128, NC * C], BF16)
    w_r = sb("w_r", [128, NC * C], BF16)
    w_k = sb("w_k", [128, NC * C], BF16)
    w_v = sb("w_v", [128, NC * C], BF16)
    mur_s = sb("mur_s", [128, NC], F32)
    muk_s = sb("muk_s", [128, NC], F32)
    murm1 = sb("murm1", [128, NC], F32)
    mukm1 = sb("mukm1", [128, NC], F32)
    murm1_b = sb("murm1_b", [128, NC * 128], BF16)
    mukm1_b = sb("mukm1_b", [128, NC * 128], BF16)
    ident = sb("ident", [128, 128], BF16)
    ones11 = sb("ones11", [1, 1], BF16)

    x_t = [sb(f"x_t{i}", [128, C], F32) for i in range(2)]
    ssum = sb("ssum", [128, 2 * NJ], F32)
    ssq = sb("ssq", [128, 2 * NJ], F32)
    mean = sb("mean", [128, 2 * NJ], F32)
    vtmp = sb("vtmp", [128, 2 * NJ], F32)
    rstd = sb("rstd", [128, 2 * NJ], F32)
    nmr = sb("nmr", [128, 2 * NJ], F32)
    xn_tm = [sb(f"xn_tm{i}", [128, C], BF16) for i in range(4)]
    xnf = sb("xnf", [128, NC * STW], BF16)
    carry = sb("carry", [128, NC], BF16)
    r_tm = sb("r_tm", [128, NJ * C], BF16)
    t2 = [sb("t2_0", [128, NC * 128], BF16)] * 4
    xr_f = sb("xr_f", [128, NC * STW], BF16)
    xk_f = sb("xk_f", [128, NC * STW], BF16)
    k_act = sb("k_act", [128, NC * STW], BF16)
    k_rel = [sb("k_rel0", [128, STW], BF16)] * 2
    phi_tm = sb("phi_tm", [128, NJ * L], BF16)
    gx_tm = sb("gx_tm", [128, NJ * C], BF16)
    v_tm = [sb(f"v_tm{i}", [128, C], BF16) for i in range(2)]
    sig_tm = [sb(f"sig_tm{i}", [128, C], BF16) for i in range(2)]
    theta_f = sb("theta_f", [128, NL * TC], BF16)
    pacc = sb("pacc", [128, 2048], F32)
    p_bf = sb("p_bf", [128, 2048], BF16)
    kbex = sb("kbex", [128, 2048], BF16)

    PA = ps("PA", [128, 1536], F32)
    PB = ps("PB", [128, 1536], F32)
    PT = [ps(f"PT{i}", [128, 1024], BF16) for i in range(2)]

    sems = {}
    for s in ("xl0", "xl1", "wl", "ml", "dve", "act", "pe", "gp", "gs0", "gs1",
              "x2_0", "x2_1", "x2_2", "x2_3",
              "g2_0", "g2_1", "g2_2", "g2_3",
              "st1_0", "st1_1", "st1_2", "st1_3",
              "st1_4", "st1_5", "st1_6", "st1_7",
              "st2_0", "st2_1", "st2_2", "st2_3",
              "st2_4", "st2_5", "st2_6", "st2_7", "cc", "pb", "pbA"):
        sems[s] = es.enter_context(nc.semaphore(s))
    r.sems = sems

    import os
    if os.environ.get("KBUILD_VERBOSE"):
        print(f"[build] sbuf bytes remaining: {nc.sbuf_bytes_remaining}, "
              f"psum banks remaining: {nc.psum_banks_remaining}")
    with es:
        _build_body(nc, r, locals())
    nc._rec = r
    return nc


def _build_body(nc, r, v):
    sems = r.sems
    TC = v["x_d"].shape[0] - 1
    NST = TC // STW
    NG = TC // 128
    x_d, out_d, gate_d = v["x_d"], v["out_d"], v["gate_d"]
    pp_in, pp_out = v["pp_in"], v["pp_out"]
    ppA_in, ppA_out = v["ppA_in"], v["ppA_out"]
    w_th, w_ph, w_g, w_r, w_k, w_v = (v["w_th"], v["w_ph"], v["w_g"],
                                      v["w_r"], v["w_k"], v["w_v"])
    mur_s, muk_s, murm1, mukm1 = v["mur_s"], v["muk_s"], v["murm1"], v["mukm1"]
    murm1_b, mukm1_b = v["murm1_b"], v["mukm1_b"]
    ident, ones11 = v["ident"], v["ones11"]
    x_t = v["x_t"]
    ssum, ssq, mean, vtmp, rstd, nmr = (v["ssum"], v["ssq"], v["mean"],
                                        v["vtmp"], v["rstd"], v["nmr"])
    xn_tm, xnf, carry = v["xn_tm"], v["xnf"], v["carry"]
    r_tm, t2, xr_f, xk_f = v["r_tm"], v["t2"], v["xr_f"], v["xk_f"]
    k_act, k_rel, phi_tm, gx_tm = v["k_act"], v["k_rel"], v["phi_tm"], v["gx_tm"]
    v_tm, sig_tm, theta_f = v["v_tm"], v["sig_tm"], v["theta_f"]
    pacc, p_bf = v["pacc"], v["p_bf"]
    kbex = v["kbex"]
    PA, PB, PT = v["PA"], v["PB"], v["PT"]
    wtt_d, wpt_d, wgt_d = v["wtt_d"], v["wpt_d"], v["wgt_d"]
    wrt_d, wkt_d, wvt_d = v["wrt_d"], v["wkt_d"], v["wvt_d"]
    mur_d, muk_d = v["mur_d"], v["muk_d"]

    W = r.wait
    E = r.emit

    def wview(t, od):
        return t[:].rearrange("p (c o) -> p c o", c=NC)

    xnf3 = xnf[:].rearrange("p (c w) -> p c w", c=NC)
    t23 = [t2[0][:].rearrange("p (c w) -> p c w", c=NC)] * 4
    xr3 = xr_f[:].rearrange("p (c w) -> p c w", c=NC)
    xk3 = xk_f[:].rearrange("p (c w) -> p c w", c=NC)
    ka3 = k_act[:].rearrange("p (d w) -> p d w", d=NC)

    r.label = "prologue"
    for src, dst, od in ((wtt_d, w_th, L), (wpt_d, w_ph, L), (wgt_d, w_g, C),
                         (wrt_d, w_r, C), (wkt_d, w_k, C), (wvt_d, w_v, C)):
        E("gpsimd", lambda e, s=src, d=dst, o=od: e.dma_start(
            out=wview(d, o), in_=s.rearrange("(c p) o -> p c o", p=128)),
          "wl", 16, dma=True)
    E("sync", lambda e: e.dma_start(out=mur_s[:], in_=mur_d), "ml", 16)
    E("sync", lambda e: e.dma_start(out=muk_s[:], in_=muk_d), "ml", 16)
    wl_all = r.cnt["wl"]
    ml_all = r.cnt["ml"]

    E("gpsimd", lambda e: e.memset(ident[:], 0.0))
    E("gpsimd", lambda e: e.affine_select(
        out=ident[:], in_=ident[:], compare_op=ALU.not_equal, fill=1.0,
        base=0, pattern=[[-1, 128]], channel_multiplier=1), "gp")
    E("gpsimd", lambda e: e.memset(ones11[:], 1.0), "gp")
    gp_init = r.cnt["gp"]

    W("vector", sems["wl"], wl_all)
    W("vector", sems["ml"], ml_all)
    E("vector", lambda e: e.tensor_scalar(murm1[:], mur_s[:], 1.0, None,
                                          ALU.subtract), "dve")
    E("vector", lambda e: e.tensor_scalar(mukm1[:], muk_s[:], 1.0, None,
                                          ALU.subtract), "dve")
    d_mu = r.cnt["dve"]
    E("vector", lambda e: e.memset(k_rel[0][:, 0:128], 0.0), "dve")
    d_z = r.cnt["dve"]
    W("scalar", sems["dve"], d_z)
    for cc in range(NC):
        E("scalar", lambda e, cc=cc: e.activation(
            out=murm1_b[:, cc * 128:(cc + 1) * 128], in_=k_rel[0][:, 0:128],
            func=AF.Identity, bias=murm1[:, cc:cc + 1]), "act")
        E("scalar", lambda e, cc=cc: e.activation(
            out=mukm1_b[:, cc * 128:(cc + 1) * 128], in_=k_rel[0][:, 0:128],
            func=AF.Identity, bias=mukm1[:, cc:cc + 1]), "act")
    a_mub = r.cnt["act"]

    E("sync", lambda e: e.dma_start(out=x_t[0][0:1, :], in_=x_d[0:1, :]),
      "xl0", 16)
    W("vector", sems["xl0"], 16)
    E("vector", lambda e: e.reduce_sum(ssum[0:1, 0:1], x_t[0][0:1, :], axis=AX.X), "dve")
    E("vector", lambda e: e.tensor_mul(xn_tm[3][0:1, :], x_t[0][0:1, :], x_t[0][0:1, :]), "dve")
    E("vector", lambda e: e.reduce_sum(ssq[0:1, 0:1], xn_tm[3][0:1, 0:1024], axis=AX.X), "dve")
    E("vector", lambda e: e.tensor_scalar(mean[0:1, 0:1], ssum[0:1, 0:1], 1.0 / C, None, ALU.mult), "dve")
    E("vector", lambda e: e.tensor_mul(vtmp[0:1, 1:2], mean[0:1, 0:1], mean[0:1, 0:1]), "dve")
    E("vector", lambda e: e.tensor_scalar(vtmp[0:1, 0:1], ssq[0:1, 0:1], 1.0 / C, EPS, ALU.mult, ALU.add), "dve")
    E("vector", lambda e: e.tensor_sub(vtmp[0:1, 0:1], vtmp[0:1, 0:1], vtmp[0:1, 1:2]), "dve")
    d_c1 = r.cnt["dve"]
    W("scalar", sems["dve"], d_c1)
    E("scalar", lambda e: e.activation(out=vtmp[0:1, 1:2], in_=vtmp[0:1, 0:1], func=AF.Sqrt), "act")
    a_c1 = r.cnt["act"]
    W("vector", sems["act"], a_c1)
    E("vector", lambda e: e.reciprocal(rstd[0:1, 0:1], vtmp[0:1, 1:2]), "dve")
    E("vector", lambda e: e.scalar_tensor_tensor(nmr[0:1, 0:1], mean[0:1, 0:1], -1.0, rstd[0:1, 0:1], ALU.mult, ALU.mult), "dve")
    d_c2 = r.cnt["dve"]
    W("scalar", sems["dve"], d_c2)
    E("scalar", lambda e: e.activation(out=xn_tm[3][0:1, :], in_=x_t[0][0:1, :], func=AF.Identity,
                                       scale=rstd[0:1, 0:1], bias=nmr[0:1, 0:1]), "act")
    a_c2 = r.cnt["act"]
    W("tensor", sems["act"], a_c2)
    W("tensor", sems["gp"], gp_init)
    for cc in range(NC):
        E("tensor", lambda e, cc=cc: e.matmul(
            PA[:, cc:cc + 1], xn_tm[3][0:1, cc * 128:(cc + 1) * 128],
            ones11[0:1, 0:1], start=True, stop=True),
          "pe", 1 if cc == NC - 1 else None)
    pe_carry = r.cnt["pe"]
    W("vector", sems["pe"], pe_carry)
    E("vector", lambda e: e.tensor_copy(carry[:, 0:NC], PA[:, 0:NC]), "dve")
    d_carry0 = r.cnt["dve"]

    xl_cnt = [16]
    xn_done = {}
    xc_done = {}
    tr_done = {}
    ev_done = {}
    ldx_done = {}

    def emit_xload(g):
        if g >= NG:
            return
        if g - 2 >= 0:
            W("sync", sems["act"], xc_done[g - 2])
        elif g == 0:
            W("sync", sems["act"], a_c2)
        sname = f"xl{g % 2}"
        E("sync", lambda e, g=g: e.dma_start(
            out=x_t[g % 2][:], in_=x_d[1 + g * 128: 1 + (g + 1) * 128, :]),
          sname, 16)
        ldx_done[g] = (sname, r.cnt[sname])

    emit_xload(0)
    emit_xload(1)

    theta_ev = {}
    phi_ev = {}
    gx_ev = {}
    mix_done = {}
    rmm_done = {}
    rev_done = {}
    kdd_done = {}
    krel_done = {}
    ksq_done = {}
    vmm_done = {}
    vev_done = {}
    ffn_done = {}
    gate_done = {}
    spill_done = {}
    pmm_done = {}
    pdr_done = {}
    th_mm_done = {}
    pg_mm_done = {}

    gs_cnt = [0]

    a_stats_d = {}
    xr_done = {}
    xk_done = {}
    t_done = {}

    def emit_ln_chunk(st, j):
        par = st % 2
        sc = par * NJ + j
        g = st * NJ + j
        xb = x_t[g % 2]
        W("scalar", sems[ldx_done[g][0]], ldx_done[g][1])
        if g == 3:
            W("scalar", sems["pe"], pe_carry)
        if g - 4 >= 0:
            W("scalar", sems["pe"], tr_done[g - 4])
        E("scalar", lambda e, xb=xb, g=g, sc=sc: e.activation(
            out=xn_tm[g % 4][:], in_=xb[:], func=AF.Square,
            accum_out=ssq[:, sc:sc + 1]), "act")
        W("vector", sems[ldx_done[g][0]], ldx_done[g][1])
        E("vector", lambda e, xb=xb, sc=sc: e.reduce_sum(
            ssum[:, sc:sc + 1], xb[:], axis=AX.X), "dve", selfdep=False)
        E("vector", lambda e, sc=sc: e.tensor_scalar(
            nmr[:, sc:sc + 1], ssum[:, sc:sc + 1], -1.0 / C, None,
            ALU.mult), "dve")
        d_nm = r.cnt["dve"]
        W("scalar", sems["dve"], d_nm)
        E("scalar", lambda e, xb=xb, g=g, sc=sc: e.activation(
            out=xn_tm[g % 4][:], in_=xb[:], func=AF.Identity,
            bias=nmr[:, sc:sc + 1]), "act")
        xc_done[g] = r.cnt["act"]
        emit_xload(g + 2)
        if j == NJ - 1:
            a_stats_d[st] = r.cnt["act"]

    def emit_rstd_scale(st):
        par = st % 2
        sl = slice(par * NJ, par * NJ + NJ)
        W("vector", sems["act"], a_stats_d[st])
        E("vector", lambda e, sl=sl: e.tensor_scalar(
            vtmp[:, sl], ssq[:, sl], 1.0 / C, EPS, ALU.mult, ALU.add), "dve",
          selfdep=False)
        E("vector", lambda e, sl=sl: e.scalar_tensor_tensor(
            rstd[:, sl], nmr[:, sl], -1.0, nmr[:, sl], ALU.mult, ALU.mult),
          "dve")
        E("vector", lambda e, sl=sl: e.tensor_add(
            vtmp[:, sl], vtmp[:, sl], rstd[:, sl]), "dve")
        d_var = r.cnt["dve"]
        W("scalar", sems["dve"], d_var)
        E("scalar", lambda e, sl=sl: e.activation(
            out=rstd[:, sl], in_=vtmp[:, sl], func=AF.Sqrt), "act",
          selfdep=False)
        a_sqrt = r.cnt["act"]
        W("vector", sems["act"], a_sqrt)
        E("vector", lambda e, sl=sl: e.reciprocal(rstd[:, sl], rstd[:, sl]),
          "dve")
        for j in range(NJ):
            g = st * NJ + j
            sc = par * NJ + j
            W("vector", sems["act"], xc_done[g])
            E("vector", lambda e, g=g, sc=sc: e.tensor_scalar(
                xn_tm[g % 4][:], xn_tm[g % 4][:], rstd[:, sc:sc + 1], None,
                ALU.mult), "dve")
            xn_done[g] = r.cnt["dve"]

    def emit_T_chunk(st, j):
        g = st * NJ + j
        W("tensor", sems["dve"], xn_done[g])
        if g - 2 >= 0:
            W("tensor", sems["dve"], ev_done[g - 2])
        elif g == 0:
            W("tensor", sems["dve"], d_carry0)
        for cc in range(NC):
            E("tensor", lambda e, g=g, cc=cc: e.transpose(
                PT[g % 2][:, cc * 128:(cc + 1) * 128],
                xn_tm[g % 4][:, cc * 128:(cc + 1) * 128], ident[:]),
              "pe", 1 if cc == NC - 1 else None)
        tr_done[g] = r.cnt["pe"]
        W("vector", sems["pe"], tr_done[g])
        if st > 0 and j == 0:
            W("vector", sems["pe"], pg_mm_done[(st - 1, NJ - 1)])
        E("vector", lambda e, g=g, j=j: e.tensor_copy(
            xnf3[:, :, j * 128:(j + 1) * 128],
            PT[g % 2][:].rearrange("p (c t) -> p c t", c=NC)), "dve",
          selfdep=False)
        ev_done[g] = r.cnt["dve"]

    def emit_mix_chunk(st, j):
        g = st * NJ + j
        tb = t23[j % 4]
        mb = murm1_b[:].rearrange("p (c w) -> p c w", c=NC)
        kb = mukm1_b[:].rearrange("p (c w) -> p c w", c=NC)
        if st == 0 and j == 0:
            W("vector", sems["act"], a_mub)
            W("vector", sems["wl"], wl_all)
            W("vector", sems["ml"], ml_all)
        lo = j * 128
        if j == 0:
            E("vector", lambda e, tb=tb: e.tensor_sub(
                tb[:, :, 0:1], xnf3[:, :, 0:1],
                carry[:].rearrange("p (c w) -> p c w", w=1)), "dve")
            E("vector", lambda e, tb=tb: e.tensor_sub(
                tb[:, :, 1:128], xnf3[:, :, 1:128], xnf3[:, :, 0:127]), "dve")
        else:
            E("vector", lambda e, tb=tb, lo=lo: e.tensor_sub(
                tb[:, :, 0:128], xnf3[:, :, lo:lo + 128],
                xnf3[:, :, lo - 1:lo + 127]), "dve")
        if j == NJ - 1:
            E("vector", lambda e: e.tensor_copy(
                carry[:].rearrange("p (c w) -> p c w", w=1),
                xnf3[:, :, STW - 1:STW]), "dve")
        E("vector", lambda e, tb=tb, mb=mb, lo=lo: e.tensor_mul(
            xr3[:, :, lo:lo + 128], tb[:, :, 0:128], mb), "dve")
        E("vector", lambda e, lo=lo: e.tensor_add(
            xr3[:, :, lo:lo + 128], xr3[:, :, lo:lo + 128],
            xnf3[:, :, lo:lo + 128]), "dve")
        xr_done[(st, j)] = r.cnt["dve"]
        E("vector", lambda e, tb=tb, kb=kb, lo=lo: e.tensor_mul(
            xk3[:, :, lo:lo + 128], tb[:, :, 0:128], kb), "dve")
        E("vector", lambda e, lo=lo: e.tensor_add(
            xk3[:, :, lo:lo + 128], xk3[:, :, lo:lo + 128],
            xnf3[:, :, lo:lo + 128]), "dve")
        xk_done[(st, j)] = r.cnt["dve"]

    for j in range(NJ):
        emit_ln_chunk(0, j)
    emit_rstd_scale(0)
    for j in range(NJ):
        emit_T_chunk(0, j)
    for j in range(NJ):
        emit_mix_chunk(0, j)

    pacc4 = pacc[:].rearrange("p (lc hh o) -> p lc hh o", lc=2, hh=2)
    ppA_in4 = ppA_in.ap().rearrange("(lc p) (hh o) -> p lc hh o", p=128, hh=2)
    hA0 = ppA_out.ap()[0:L, :].rearrange("(lc p) (hh o) -> p lc hh o", p=128, hh=2)
    hA1 = ppA_out.ap()[L:2 * L, :].rearrange("(lc p) (hh o) -> p lc hh o", p=128, hh=2)
    pbA_done = [0]
    pb_pbf_v = [0]
    d_pacc_free_v = [None]

    def emit_exchange_A():
        W("gpsimd", sems["dve"], pdr_done[NST - 2])
        E("gpsimd", lambda e: e.dma_start(out=ppA_in4, in_=pacc4), "pbA", 16, dma=True)
        W("gpsimd", sems["pbA"], r.cnt["pbA"])
        E("gpsimd", lambda e: e.collective_compute(
            "AllGather", ALU.bypass,
            replica_groups=[[0, 1], [2, 3], [4, 5], [6, 7]],
            ins=[ppA_in.ap()], outs=[ppA_out.ap()]), "cc", 1, dma=True)
        W("gpsimd", sems["cc"], r.cnt["cc"])
        E("gpsimd", lambda e: e.dma_start(out=pacc4, in_=hA0), "pbA", 16, dma=True)
        W("gpsimd", sems["pbA"], r.cnt["pbA"])
        E("gpsimd", lambda e: e.dma_start(out=pacc4, in_=hA1, accum_op=ALU.add), "pbA", 16, dma=True)
        pbA_done[0] = r.cnt["pbA"]

    for st in range(NST):
        r.label = f"st{st}"
        par = st % 2
        if NST >= 2 and st == NST - 1:
            emit_exchange_A()

        W("tensor", sems["wl"], wl_all)
        if st > 0:
            W("tensor", sems["dve"], pdr_done[st - 1])
        W("tensor", sems["dve"], ev_done[st * NJ + NJ - 1])
        for lc in range(NL):
            for cc in range(NC):
                E("tensor", lambda e, lc=lc, cc=cc: e.matmul(
                    PA[:, lc * 512:(lc + 1) * 512],
                    w_th[:, cc * L + lc * 128: cc * L + (lc + 1) * 128],
                    xnf3[:, cc, :], start=(cc == 0), stop=(cc == NC - 1)),
                  "pe", 1 if (lc == NL - 1 and cc == NC - 1) else None)
        th_mm_done[st] = r.cnt["pe"]
        W("scalar", sems["pe"], th_mm_done[st])
        E("scalar", lambda e, st=st: e.activation(
            out=theta_f[:].rearrange("p (l T) -> p l T", l=NL)[:, :, st * STW:(st + 1) * STW],
            in_=PA[:, 0:1024].rearrange("p (l t) -> p l t", l=NL),
            func=AF.Copy), "act", selfdep=False)
        theta_ev[st] = r.cnt["act"]

        def emit_r_group(j, theta_wait, extra_dve=None):
            W("tensor", sems["dve"], xr_done[(st, j)])
            if extra_dve:
                W("tensor", sems["dve"], extra_dve)
            if theta_wait is not None:
                W("tensor", sems["act"], theta_wait)
            elif j > 0:
                W("tensor", sems["act"], rev_done[(st, j - 1)])
            for cc in range(NC):
                for hh in range(2):
                    E("tensor", lambda e, j=j, cc=cc, hh=hh: e.matmul(
                        PA[:, hh * 512:(hh + 1) * 512],
                        xr3[:, cc, j * 128:(j + 1) * 128],
                        w_r[:, cc * C + hh * 512: cc * C + (hh + 1) * 512],
                        start=(cc == 0), stop=(cc == NC - 1)),
                      "pe", 1 if (cc == NC - 1 and hh == 1) else None)
            rmm_done[(st, j)] = r.cnt["pe"]
            W("scalar", sems["pe"], rmm_done[(st, j)])
            if st > 0:
                fd = ffn_done[(st - 1, j)]
                W("scalar", sems[fd[0]], fd[1])
            E("scalar", lambda e, j=j: e.activation(
                out=r_tm[:, j * C:(j + 1) * C], in_=PA[:, 0:1024],
                func=AF.Sigmoid), "act", selfdep=False)
            rev_done[(st, j)] = r.cnt["act"]

        def emit_k_group(dd, extra_dve=None):
            reg = PA[:, 1024:1536] if dd % 2 == 0 else PB[:, 1024:1536]
            W("tensor", sems["dve"], xk_done[(st, NJ - 1)])
            if extra_dve:
                W("tensor", sems["dve"], extra_dve)
            if dd == 0:
                W("tensor", sems["act"], phi_ev[(st, NJ - 1)])
            elif dd >= 2:
                W("tensor", sems["dve"], krel_done[(st, dd - 2)])
            for cc in range(NC):
                E("tensor", lambda e, reg=reg, dd=dd, cc=cc: e.matmul(
                    reg, w_k[:, cc * C + dd * 128: cc * C + (dd + 1) * 128],
                    xk3[:, cc, :], start=(cc == 0), stop=(cc == NC - 1)),
                  "pe", 1 if cc == NC - 1 else None)
            kdd_done[(st, dd)] = r.cnt["pe"]
            W("vector", sems["pe"], kdd_done[(st, dd)])
            E("vector", lambda e, reg=reg, dd=dd: e.tensor_scalar(
                k_rel[dd % 2][:], reg, 0.0, None, ALU.max), "dve")
            krel_done[(st, dd)] = r.cnt["dve"]
            E("vector", lambda e, dd=dd: e.tensor_mul(
                ka3[:, dd, :], k_rel[dd % 2][:], k_rel[dd % 2][:]), "dve")
            ksq_done[(st, dd)] = r.cnt["dve"]

        last = (NST >= 2 and st == NST - 1)
        d_kb = [0]

        for j in range(NJ):
            if st > 0 and j == 0:
                W("tensor", sems["dve"], pdr_done[st - 1])
                W("tensor", sems["act"], vev_done.get((st - 1, NJ - 1), 0))
            if j > 0:
                W("tensor", sems["act"], phi_ev[(st, j - 1)])
                W("tensor", sems["act"], gx_ev[(st, j - 1)])
            for cc in range(NC):
                E("tensor", lambda e, j=j, cc=cc: e.matmul(
                    PA[:, 1024:1280],
                    xnf3[:, cc, j * 128:(j + 1) * 128],
                    w_ph[:, cc * L:(cc + 1) * L],
                    start=(cc == 0), stop=(cc == NC - 1)))
                for hh in range(2):
                    E("tensor", lambda e, j=j, cc=cc, hh=hh: e.matmul(
                        PB[:, hh * 512:(hh + 1) * 512],
                        xnf3[:, cc, j * 128:(j + 1) * 128],
                        w_g[:, cc * C + hh * 512: cc * C + (hh + 1) * 512],
                        start=(cc == 0), stop=(cc == NC - 1)),
                      "pe", 1 if (cc == NC - 1 and hh == 1) else None)
            pg_mm_done[(st, j)] = r.cnt["pe"]
            W("scalar", sems["pe"], pg_mm_done[(st, j)])
            E("scalar", lambda e, j=j: e.activation(
                out=phi_tm[:, j * L:(j + 1) * L], in_=PA[:, 1024:1280],
                func=AF.Copy), "act", selfdep=False)
            phi_ev[(st, j)] = r.cnt["act"]
            E("scalar", lambda e, j=j: e.activation(
                out=gx_tm[:, j * C:(j + 1) * C], in_=PB[:, 0:1024],
                func=AF.Copy), "act", selfdep=False)
            gx_ev[(st, j)] = r.cnt["act"]

            if not last:
                emit_r_group(j, theta_ev[st] if j == 0 else None)

            if st + 1 < NST:
                emit_ln_chunk(st + 1, j)

        if last:
            W("tensor", sems["act"], theta_ev[st])
            W("tensor", sems["act"], phi_ev[(st, NJ - 1)])
            W("tensor", sems["act"], gx_ev[(st, NJ - 1)])
            pregs0 = (PA[:, 0:512], PA[:, 512:1024], PA[:, 1024:1536], PB[:, 0:512])
            for j in range(NJ):
                for q, reg in enumerate(pregs0):
                    lc, hh = q // 2, q % 2
                    E("tensor", lambda e, reg=reg, j=j, lc=lc, hh=hh: e.matmul(
                        reg, phi_tm[:, j * L + lc * 128: j * L + (lc + 1) * 128],
                        gx_tm[:, j * C + hh * 512: j * C + (hh + 1) * 512],
                        start=(j == 0), stop=(j == NJ - 1)),
                      "pe", 1 if (j == NJ - 1 and q == 3) else None)
            pmm_done[st] = r.cnt["pe"]
            W("vector", sems["pe"], pmm_done[st])
            E("vector", lambda e: e.tensor_scalar(
                kbex[:, 0:1536], PA[:, 0:1536], SCALE, None, ALU.mult), "dve")
            E("vector", lambda e: e.tensor_scalar(
                kbex[:, 1536:2048], PB[:, 0:512], SCALE, None, ALU.mult), "dve")
            d_kb[0] = r.cnt["dve"]
            kbex4 = kbex[:].rearrange("p (lc hh o) -> p lc hh o", lc=2, hh=2)
            pp_in4_ = pp_in.ap().rearrange("(lc p) (hh o) -> p lc hh o", p=128, hh=2)
            W("gpsimd", sems["dve"], d_kb[0])
            E("gpsimd", lambda e: e.dma_start(out=pp_in4_, in_=kbex4), "pb", 16, dma=True)
            W("gpsimd", sems["pb"], r.cnt["pb"])
            E("gpsimd", lambda e: e.collective_compute(
                "AllGather", ALU.bypass,
                replica_groups=[[0, 1], [2, 3], [4, 5], [6, 7]],
                ins=[pp_in.ap()], outs=[pp_out.ap()]), "cc", 1, dma=True)
            cc_b_last = r.cnt["cc"]
            pbf4_l = p_bf[:].rearrange("p (lc hh o) -> p lc hh o", lc=2, hh=2)
            hB0 = pp_out.ap()[0:L, :].rearrange("(lc p) (hh o) -> p lc hh o", p=128, hh=2)
            hB1 = pp_out.ap()[L:2 * L, :].rearrange("(lc p) (hh o) -> p lc hh o", p=128, hh=2)
            W("gpsimd", sems["pbA"], pbA_done[0])
            E("gpsimd", lambda e: e.dma_start(out=p_bf[:], in_=pacc[:]),
              "pb", 16, dma=True)
            d_pacc_free_v[0] = ("pb", r.cnt["pb"])
            W("gpsimd", sems["pb"], r.cnt["pb"])
            W("gpsimd", sems["cc"], cc_b_last)
            E("gpsimd", lambda e, hB0=hB0: e.dma_start(
                out=pbf4_l, in_=hB0, accum_op=ALU.add), "pb", 16, dma=True)
            W("gpsimd", sems["pb"], r.cnt["pb"])
            E("gpsimd", lambda e, hB1=hB1: e.dma_start(
                out=pbf4_l, in_=hB1, accum_op=ALU.add), "pb", 16, dma=True)
            pb_pbf_v[0] = r.cnt["pb"]
            for j in range(NJ):
                emit_r_group(j, None, extra_dve=d_kb[0])

        for dd in range(NC):
            emit_k_group(dd, extra_dve=(d_kb[0] if (last and dd == 0) else None))
        if st + 1 < NST:
            emit_rstd_scale(st + 1)

        pregs = (PA[:, 0:512], PA[:, 512:1024], PA[:, 1024:1536], PB[:, 0:512])
        for j in range(NJ):
            g = st * NJ + j
            if not last:
                if j == 0:
                    W("tensor", sems["act"], rev_done[(st, NJ - 1)])
                    W("tensor", sems["dve"], krel_done[(st, NC - 2)])
                    W("tensor", sems["act"], gx_ev[(st, NJ - 1)])
                for q, reg in enumerate(pregs):
                    lc, hh = q // 2, q % 2
                    E("tensor", lambda e, reg=reg, j=j, lc=lc, hh=hh: e.matmul(
                        reg, phi_tm[:, j * L + lc * 128: j * L + (lc + 1) * 128],
                        gx_tm[:, j * C + hh * 512: j * C + (hh + 1) * 512],
                        start=(j == 0), stop=(j == NJ - 1)),
                      "pe", 1 if (j == NJ - 1 and q == 3) else None)
            if j == NJ - 1 and not last:
                pmm_done[st] = r.cnt["pe"]
                if not (NST >= 2 and st == NST - 1):
                    W("vector", sems["pe"], pmm_done[st])
                    if st == 0:
                        E("vector", lambda e: e.tensor_scalar(
                            pacc[:, 0:1536], PA[:, 0:1536], SCALE, None, ALU.mult), "dve")
                        E("vector", lambda e: e.tensor_scalar(
                            pacc[:, 1536:2048], PB[:, 0:512], SCALE, None, ALU.mult), "dve")
                    else:
                        E("vector", lambda e: e.scalar_tensor_tensor(
                            pacc[:, 0:1536], PA[:, 0:1536], SCALE, pacc[:, 0:1536],
                            ALU.mult, ALU.add), "dve")
                        E("vector", lambda e: e.scalar_tensor_tensor(
                            pacc[:, 1536:2048], PB[:, 0:512], SCALE, pacc[:, 1536:2048],
                            ALU.mult, ALU.add), "dve")
                    pdr_done[st] = r.cnt["dve"]

            if st + 1 < NST:
                emit_T_chunk(st + 1, j)
                if j >= 1:
                    emit_mix_chunk(st + 1, j - 1)

            if j == 0:
                W("tensor", sems["dve"], max(ksq_done[(st, dd)] for dd in range(NC)))
                W("tensor", sems["act"], gx_ev[(st, NJ - 1)])
            else:
                W("tensor", sems["act"], vev_done[(st, j - 1)])
            for dd in range(NC):
                for hh in range(2):
                    E("tensor", lambda e, j=j, dd=dd, hh=hh: e.matmul(
                        PB[:, 512 + hh * 512: 512 + (hh + 1) * 512],
                        ka3[:, dd, j * 128:(j + 1) * 128],
                        w_v[:, dd * C + hh * 512: dd * C + (hh + 1) * 512],
                        start=(dd == 0), stop=(dd == NC - 1)),
                      "pe", 1 if (dd == NC - 1 and hh == 1) else None)
            vmm_done[(st, j)] = r.cnt["pe"]
            W("scalar", sems["pe"], vmm_done[(st, j)])
            if g - 2 >= 0:
                gd = gate_done[(st, j - 2)] if j - 2 >= 0 else gate_done[(st - 1, j + 2)]
                W("scalar", sems[gd[0]], gd[1])
            E("scalar", lambda e, g=g: e.activation(
                out=v_tm[g % 2][:], in_=PB[:, 512:1536], func=AF.Copy),
              "act", selfdep=False)
            vev_done[(st, j)] = r.cnt["act"]
            feng, fsem = ("vector", "dve") if last else ("gpsimd", "gp")
            W(feng, sems["act"], vev_done[(st, j)])
            W(feng, sems["act"], rev_done[(st, j)])
            E(feng, lambda e, g=g, j=j: e.tensor_mul(
                v_tm[g % 2][:], r_tm[:, j * C:(j + 1) * C], v_tm[g % 2][:]),
              fsem, selfdep=False)
            ffn_done[(st, j)] = (fsem, r.cnt[fsem])
            W("scalar", sems[fsem], ffn_done[(st, j)][1])
            if g - 2 >= 0:
                W("scalar", sems[spill_done[g - 2][0]], spill_done[g - 2][1])
            E("scalar", lambda e, g=g: e.activation(
                out=sig_tm[g % 2][:], in_=v_tm[g % 2][:], func=AF.Sigmoid),
              "act", selfdep=False)
            a_sig = r.cnt["act"]
            W(feng, sems["act"], a_sig)
            E(feng, lambda e, g=g: e.tensor_mul(
                sig_tm[g % 2][:], v_tm[g % 2][:], sig_tm[g % 2][:]), fsem)
            gate_done[(st, j)] = (fsem, r.cnt[fsem])
            W("sync", sems[gate_done[(st, j)][0]], gate_done[(st, j)][1])
            sname = f"gs{g % 2}"
            E("sync", lambda e, g=g: e.dma_start(
                out=gate_d[g * 128:(g + 1) * 128, :], in_=sig_tm[g % 2][:]),
              sname, 16)
            spill_done[g] = (sname, r.cnt[sname])

        if st + 1 < NST:
            emit_mix_chunk(st + 1, NJ - 1)


    r.label = "exchange"
    ksend = k_act[:, 0:2048]
    pp_in4 = pp_in.ap().rearrange("(lc p) (hh o) -> p lc hh o", p=128, hh=2)
    ksend4 = ksend.rearrange("p (lc hh o) -> p lc hh o", lc=2, hh=2)
    pbf4 = p_bf[:].rearrange("p (lc hh o) -> p lc hh o", lc=2, hh=2)
    h0 = pp_out.ap()[0:L, :].rearrange("(lc p) (hh o) -> p lc hh o", p=128, hh=2)
    h1 = pp_out.ap()[L:2 * L, :].rearrange("(lc p) (hh o) -> p lc hh o", p=128, hh=2)
    if NST >= 2:
        pb_pbf = pb_pbf_v[0]
        d_pacc_free = d_pacc_free_v[0]
    else:
        W("vector", sems["pe"], vmm_done[(NST - 1, NJ - 1)])
        E("vector", lambda e: e.tensor_copy(ksend, pacc[:]), "dve")
        d_scaled = r.cnt["dve"]
        d_pacc_free = ("dve", d_scaled)
        W("gpsimd", sems["dve"], d_scaled)
        E("gpsimd", lambda e: e.dma_start(out=pp_in4, in_=ksend4), "pb", 16, dma=True)
        W("gpsimd", sems["pb"], r.cnt["pb"])
        E("gpsimd", lambda e: e.collective_compute(
            "AllGather", ALU.bypass,
            replica_groups=[[0, 1], [2, 3], [4, 5], [6, 7]],
            ins=[pp_in.ap()], outs=[pp_out.ap()]), "cc", 1, dma=True)
        cc_b = r.cnt["cc"]
        W("gpsimd", sems["cc"], cc_b)
        E("gpsimd", lambda e: e.dma_start(out=pbf4, in_=h0), "pb", 16, dma=True)
        W("gpsimd", sems["pb"], r.cnt["pb"])
        E("gpsimd", lambda e: e.dma_start(out=pbf4, in_=h1, accum_op=ALU.add), "pb", 16, dma=True)
        pb_pbf = r.cnt["pb"]

    sgt = []
    for buf in (xnf, xr_f, xk_f, k_act):
        bc = buf[:].bitcast(F32)
        sgt.append(bc[:, 0:1024])
        sgt.append(bc[:, 1024:2048])
    g2 = [r_tm[:, i * 1024:(i + 1) * 1024] for i in range(4)]
    pacc_f = pacc[:]
    x2 = [x_t[0][:], x_t[1][:], pacc_f[:, 0:1024], pacc_f[:, 1024:2048]]
    g2_done = {}
    smm_done = {}
    op1_done = {}
    op2_done = {}
    x2_done = {}
    store_done = {}

    def emit_g2load(g):
        if g >= NG:
            return
        if g == 0:
            W("scalar", sems["gs0"], r.cnt.get("gs0", 0))
            W("scalar", sems["gs1"], r.cnt.get("gs1", 0))
        if g - 4 >= 0:
            W("scalar", sems["dve"], op1_done[g - 4])
        sname = f"g2_{g % 4}"
        E("scalar", lambda e, g=g: e.dma_start(
            out=g2[g % 4], in_=gate_d[g * 128:(g + 1) * 128, :]),
          sname, 16, dma=True)
        g2_done[g] = (sname, r.cnt[sname])

    def emit_x2load(g):
        if g >= NG:
            return
        if g == 0:
            W("scalar", sems["pbA"], pbA_done[0])
            W("scalar", sems[d_pacc_free[0]], d_pacc_free[1])
        if g - 4 >= 0:
            W("scalar", sems[store_done[g - 4][0]], store_done[g - 4][1])
        sname = f"x2_{g % 4}"
        E("scalar", lambda e, g=g: e.dma_start(
            out=x2[g % 4], in_=x_d[1 + g * 128: 1 + (g + 1) * 128, :]),
          sname, 16, dma=True)
        x2_done[g] = (sname, r.cnt[sname])

    for g in range(4):
        emit_g2load(g)
    for g in range(4):
        emit_x2load(g)

    p2regs = [(PA[:, 0:512], PA[:, 512:1024]),
              (PA[:, 1024:1536], PB[:, 0:512]),
              (PB[:, 512:1024], PB[:, 1024:1536])]
    for g in range(NG):
        r.label = f"p2g{g}"
        rpair = p2regs[g % 3]
        if g == 0:
            W("tensor", sems["pb"], pb_pbf)
            W("tensor", sems["act"], theta_ev[NST - 1])
            W("tensor", sems["act"], vev_done[(NST - 1, NJ - 1)])
            W("tensor", sems["dve"], max(ksq_done[(NST - 1, dd)] for dd in range(NC)))
        if g - 3 >= 0:
            W("tensor", sems["dve"], op1_done[g - 3])
        for hh in range(2):
            for lc in range(NL):
                E("tensor", lambda e, rg=rpair[hh], g=g, lc=lc, hh=hh: e.matmul(
                    rg,
                    theta_f[:, lc * TC + g * 128: lc * TC + (g + 1) * 128],
                    p_bf[:, (lc * 2 + hh) * 512:(lc * 2 + hh + 1) * 512],
                    start=(lc == 0), stop=(lc == NL - 1)),
                  "pe", 1 if (lc == NL - 1 and hh == 1) else None)
        smm_done[g] = r.cnt["pe"]
        W("vector", sems["pe"], smm_done[g])
        W("vector", sems[g2_done[g][0]], g2_done[g][1])
        if g - 8 >= 0:
            W("vector", sems[store_done[g - 8][0]], store_done[g - 8][1])
        for hh in range(2):
            E("vector", lambda e, rg=rpair[hh], g=g, hh=hh: e.tensor_mul(
                sgt[g % 8][:, hh * 512:(hh + 1) * 512], rg,
                g2[g % 4][:, hh * 512:(hh + 1) * 512]), "dve")
        op1_done[g] = r.cnt["dve"]
        emit_g2load(g + 4)
        eng2 = "vector" if g % 2 else "gpsimd"
        sem2 = "dve" if g % 2 else "gp"
        if eng2 == "gpsimd":
            W(eng2, sems["dve"], op1_done[g])
        W(eng2, sems[x2_done[g][0]], x2_done[g][1])
        E(eng2, lambda e, g=g: e.tensor_add(
            sgt[g % 8][:], sgt[g % 8][:], x2[g % 4]), sem2)
        op2_done[g] = (r.cnt[sem2], sem2)
        cntv, semn = op2_done[g]
        W("sync", sems[semn], cntv)
        s1 = f"st1_{g % 8}"
        if g - 8 >= 0:
            W("sync", sems[store_done[g - 8][0]], store_done[g - 8][1])
        E("sync", lambda e, g=g: e.dma_start(
            out=out_d[g * 128:(g + 1) * 128, :], in_=sgt[g % 8][:]), s1, 16)
        store_done[g] = (s1, r.cnt[s1])
        emit_x2load(g + 4)

    if v.get("dbg"):
        for i in range(8):
            W("gpsimd", sems[f"st1_{i}"], r.cnt.get(f"st1_{i}", 0))
        E("gpsimd", lambda e: e.dma_start(out=v["dbg_st"][:, 0:8], in_=ssum[:]), "pb", 16, dma=True)
        E("gpsimd", lambda e: e.dma_start(out=v["dbg_st"][:, 8:16], in_=ssq[:]), "pb", 16, dma=True)
        E("gpsimd", lambda e: e.dma_start(out=v["dbg_st"][:, 16:24], in_=vtmp[:]), "pb", 16, dma=True)
        E("gpsimd", lambda e: e.dma_start(out=v["dbg_st"][:, 24:32], in_=rstd[:]), "pb", 16, dma=True)
        E("gpsimd", lambda e: e.dma_start(out=v["dbg_st"][:, 32:40], in_=nmr[:]), "pb", 16, dma=True)
        E("gpsimd", lambda e: e.dma_start(out=v["dbg_th"], in_=theta_f[:]), "pb", 16, dma=True)
        E("gpsimd", lambda e: e.dma_start(out=v["dbg_pac"], in_=pacc[:]), "pb", 16, dma=True)
        E("gpsimd", lambda e: e.dma_start(out=v["dbg_gate"], in_=gate_d), "pb", 16, dma=True)
        E("gpsimd", lambda e: e.dma_start(out=v["dbg_xnf"], in_=xnf[:]), "pb", 16, dma=True)
        W("gpsimd", sems["pb"], r.cnt["pb"])

    for i in range(8):
        W("gpsimd", sems[f"st1_{i}"], r.cnt.get(f"st1_{i}", 0))

    with nc.Block() as block:
        for eng in ENGINES:
            def body(e, eng=eng):
                for f in r.ops[eng]:
                    f(e)
            getattr(block, eng)(body)


def kernel(L0_attn_x, L0_attn_W_theta, L0_attn_W_phi, L0_attn_W_g,
           L0_ffn_W_r, L0_ffn_W_k, L0_ffn_W_v, L0_ffn_mu_r, L0_ffn_mu_k,
           _dbg=False, _results=None):
    x = np.asarray(L0_attn_x, dtype=np.float32)
    B, T, C_ = x.shape
    half = T // 2
    wtt = np.ascontiguousarray(np.asarray(L0_attn_W_theta, dtype=np.float32).T)
    wpt = np.ascontiguousarray(np.asarray(L0_attn_W_phi, dtype=np.float32).T)
    wgt = np.ascontiguousarray(np.asarray(L0_attn_W_g, dtype=np.float32).T)
    wrt = np.ascontiguousarray(np.asarray(L0_ffn_W_r, dtype=np.float32).T)
    wkt = np.ascontiguousarray(np.asarray(L0_ffn_W_k, dtype=np.float32).T)
    wvt = np.ascontiguousarray(np.asarray(L0_ffn_W_v, dtype=np.float32).T)
    mur = np.ascontiguousarray(np.asarray(L0_ffn_mu_r, dtype=np.float32).reshape(NC, 128).T)
    muk = np.ascontiguousarray(np.asarray(L0_ffn_mu_k, dtype=np.float32).reshape(NC, 128).T)

    nc = build_program(half, dbg=_dbg)
    zrow = np.zeros((1, C_), np.float32)
    in_maps = []
    for c in range(8):
        b, h = c // 2, c % 2
        t0 = h * half
        prev = zrow if h == 0 else x[b, t0 - 1:t0]
        xs = np.ascontiguousarray(np.concatenate([prev, x[b, t0:t0 + half]], axis=0))
        in_maps.append({"x": xs, "wtt": wtt, "wpt": wpt, "wgt": wgt,
                        "wrt": wrt, "wkt": wkt, "wvt": wvt,
                        "mur": mur, "muk": muk})
    res = run_bass_kernel_spmd(nc, in_maps, core_ids=list(range(8)))
    if _results is not None:
        _results.append(res)
    out = np.empty((B, T, C_), np.float32)
    for c in range(8):
        b, h = c // 2, c % 2
        out[b, h * half:(h + 1) * half] = res.results[c]["out"]
    return out


# version 33 (final)
# speedup vs baseline: 1.0454x; 1.0454x over previous
import numpy as np
import concourse.bass as bass
import concourse.mybir as mybir
from concourse.bass_utils import run_bass_kernel_spmd
from contextlib import ExitStack

F32 = mybir.dt.float32
BF16 = mybir.dt.bfloat16
AF = mybir.ActivationFunctionType
ALU = mybir.AluOpType
AX = mybir.AxisListType

C = 1024
L = 256
NC = C // 128
NL = L // 128
STW = 512
NJ = STW // 128
EPS = 1e-5
SCALE = 1.0 / 32.0

ENGINES = ("sync", "gpsimd", "vector", "scalar", "tensor")


class Rec:

    def __init__(self):
        self.ops = {e: [] for e in ENGINES}
        self.meta = {e: [] for e in ENGINES}
        self.cnt = {}
        self.label = ""

    def wait(self, eng, sem, val):
        if val <= 0:
            return
        name = None
        for n, s in self.sems.items():
            if s is sem:
                name = n
                break
        self.ops[eng].append(lambda e, s=sem, v=val: e.wait_ge(s, v))
        self.meta[eng].append(("wait", name, val, self.label))

    OWN = {"vector": "dve", "scalar": "act", "gpsimd": "gp"}

    def emit(self, eng, fn, sem=None, k=1, dma=False, selfdep=True):
        if not dma and eng in self.OWN:
            own = self.OWN[eng]
            assert sem is None or sem == own, (eng, sem)
            pre = self.cnt.get(own, 0)
            if selfdep and pre > 0:
                self.ops[eng].append(
                    lambda e, s=self.sems[own], v=pre: e.wait_ge(s, v))
                self.meta[eng].append(("wait", own, pre, self.label))
            sem, k = own, 1
        if sem is None or k is None:
            self.ops[eng].append(fn)
            self.meta[eng].append(("op", None, 0, self.label))
            return None
        name = sem
        self.cnt[name] = self.cnt.get(name, 0) + k

        def wrapped(e, fn=fn, k=k):
            fn(e).then_inc(self.sems[name], k)

        self.ops[eng].append(wrapped)
        self.meta[eng].append(("op", name, k, self.label))
        return self.cnt[name]


def build_program(TC, dbg=False):
    NST = TC // STW
    NG = TC // 128

    nc = bass.Bass("TRN2", target_bir_lowering=False, debug=False)
    if dbg:
        nc.dbg_aps_list = []

    x_d = nc.dram_tensor("x", [TC + 1, C], F32, kind="ExternalInput").ap()
    wtt_d = nc.dram_tensor("wtt", [C, L], F32, kind="ExternalInput").ap()
    wpt_d = nc.dram_tensor("wpt", [C, L], F32, kind="ExternalInput").ap()
    wgt_d = nc.dram_tensor("wgt", [C, C], F32, kind="ExternalInput").ap()
    wrt_d = nc.dram_tensor("wrt", [C, C], F32, kind="ExternalInput").ap()
    wkt_d = nc.dram_tensor("wkt", [C, C], F32, kind="ExternalInput").ap()
    wvt_d = nc.dram_tensor("wvt", [C, C], F32, kind="ExternalInput").ap()
    mur_d = nc.dram_tensor("mur", [128, NC], F32, kind="ExternalInput").ap()
    muk_d = nc.dram_tensor("muk", [128, NC], F32, kind="ExternalInput").ap()
    out_d = nc.dram_tensor("out", [TC, C], F32, kind="ExternalOutput").ap()

    gate_d = nc.dram_tensor("gate_hbm", [TC, C], BF16).ap()
    if dbg:
        dbg_st = nc.dram_tensor("dbg_st", [128, 48], F32, kind="ExternalOutput").ap()
        dbg_th = nc.dram_tensor("dbg_th", [128, 2 * TC], F32, kind="ExternalOutput").ap()
        dbg_pac = nc.dram_tensor("dbg_pac", [128, 2048], F32, kind="ExternalOutput").ap()
        dbg_gate = nc.dram_tensor("dbg_gate", [TC, C], F32, kind="ExternalOutput").ap()
        dbg_xnf = nc.dram_tensor("dbg_xnf", [128, NC * STW], F32, kind="ExternalOutput").ap()
    else:
        dbg_st = dbg_th = dbg_pac = dbg_gate = dbg_xnf = None
    pp_in = nc.dram_tensor("pp_in", [L, C], BF16)
    pp_out = nc.dram_tensor("pp_out", [2 * L, C], BF16)
    ppA_in = nc.dram_tensor("ppA_in", [L, C], F32)
    ppA_out = nc.dram_tensor("ppA_out", [2 * L, C], F32)

    r = Rec()
    es = ExitStack()
    sb = lambda name, shape, dt: es.enter_context(nc.sbuf_tensor(name, shape, dt))
    ps = lambda name, shape, dt: es.enter_context(nc.psum_tensor(name, shape, dt))

    w_th = sb("w_th", [128, NC * L], BF16)
    w_ph = sb("w_ph", [128, NC * L], BF16)
    w_g = sb("w_g", [128, NC * C], BF16)
    w_r = sb("w_r", [128, NC * C], BF16)
    w_k = sb("w_k", [128, NC * C], BF16)
    w_v = sb("w_v", [128, NC * C], BF16)
    mur_s = sb("mur_s", [128, NC], F32)
    muk_s = sb("muk_s", [128, NC], F32)
    murm1 = sb("murm1", [128, NC], F32)
    mukm1 = sb("mukm1", [128, NC], F32)
    murm1_b = sb("murm1_b", [128, NC * 128], BF16)
    mukm1_b = sb("mukm1_b", [128, NC * 128], BF16)
    ident = sb("ident", [128, 128], BF16)
    ones11 = sb("ones11", [1, 1], BF16)

    x_t = [sb(f"x_t{i}", [128, C], F32) for i in range(2)]
    ssum = sb("ssum", [128, 2 * NJ], F32)
    ssq = sb("ssq", [128, 2 * NJ], F32)
    mean = sb("mean", [128, 2 * NJ], F32)
    vtmp = sb("vtmp", [128, 2 * NJ], F32)
    rstd = sb("rstd", [128, 2 * NJ], F32)
    nmr = sb("nmr", [128, 2 * NJ], F32)
    xn_tm = [sb(f"xn_tm{i}", [128, C], BF16) for i in range(4)]
    xnf = sb("xnf", [128, NC * STW], BF16)
    carry = sb("carry", [128, NC], BF16)
    r_tm = sb("r_tm", [128, NJ * C], BF16)
    t2 = [sb("t2_0", [128, NC * 128], BF16)] * 4
    xr_f = sb("xr_f", [128, NC * STW], BF16)
    xk_f = sb("xk_f", [128, NC * STW], BF16)
    k_act = sb("k_act", [128, NC * STW], BF16)
    k_rel = [sb("k_rel0", [128, STW], BF16)] * 2
    phi_tm = sb("phi_tm", [128, NJ * L], BF16)
    gx_tm = sb("gx_tm", [128, NJ * C], BF16)
    v_tm = [sb(f"v_tm{i}", [128, C], BF16) for i in range(2)]
    sig_tm = [sb(f"sig_tm{i}", [128, C], BF16) for i in range(2)]
    theta_f = sb("theta_f", [128, NL * TC], BF16)
    pacc = sb("pacc", [128, 2048], F32)
    p_bf = sb("p_bf", [128, 2048], BF16)
    kbex = sb("kbex", [128, 2048], BF16)

    PA = ps("PA", [128, 1536], F32)
    PB = ps("PB", [128, 1536], F32)
    PT = [ps(f"PT{i}", [128, 1024], BF16) for i in range(2)]

    sems = {}
    for s in ("xl0", "xl1", "wl", "ml", "dve", "act", "pe", "gp", "gs0", "gs1",
              "x2_0", "x2_1", "x2_2", "x2_3",
              "g2_0", "g2_1", "g2_2", "g2_3",
              "st1_0", "st1_1", "st1_2", "st1_3",
              "st1_4", "st1_5", "st1_6", "st1_7",
              "st2_0", "st2_1", "st2_2", "st2_3",
              "st2_4", "st2_5", "st2_6", "st2_7", "cc", "pb", "pbA"):
        sems[s] = es.enter_context(nc.semaphore(s))
    r.sems = sems

    import os
    if os.environ.get("KBUILD_VERBOSE"):
        print(f"[build] sbuf bytes remaining: {nc.sbuf_bytes_remaining}, "
              f"psum banks remaining: {nc.psum_banks_remaining}")
    with es:
        _build_body(nc, r, locals())
    nc._rec = r
    return nc


def _build_body(nc, r, v):
    sems = r.sems
    TC = v["x_d"].shape[0] - 1
    NST = TC // STW
    NG = TC // 128
    x_d, out_d, gate_d = v["x_d"], v["out_d"], v["gate_d"]
    pp_in, pp_out = v["pp_in"], v["pp_out"]
    ppA_in, ppA_out = v["ppA_in"], v["ppA_out"]
    w_th, w_ph, w_g, w_r, w_k, w_v = (v["w_th"], v["w_ph"], v["w_g"],
                                      v["w_r"], v["w_k"], v["w_v"])
    mur_s, muk_s, murm1, mukm1 = v["mur_s"], v["muk_s"], v["murm1"], v["mukm1"]
    murm1_b, mukm1_b = v["murm1_b"], v["mukm1_b"]
    ident, ones11 = v["ident"], v["ones11"]
    x_t = v["x_t"]
    ssum, ssq, mean, vtmp, rstd, nmr = (v["ssum"], v["ssq"], v["mean"],
                                        v["vtmp"], v["rstd"], v["nmr"])
    xn_tm, xnf, carry = v["xn_tm"], v["xnf"], v["carry"]
    r_tm, t2, xr_f, xk_f = v["r_tm"], v["t2"], v["xr_f"], v["xk_f"]
    k_act, k_rel, phi_tm, gx_tm = v["k_act"], v["k_rel"], v["phi_tm"], v["gx_tm"]
    v_tm, sig_tm, theta_f = v["v_tm"], v["sig_tm"], v["theta_f"]
    pacc, p_bf = v["pacc"], v["p_bf"]
    kbex = v["kbex"]
    PA, PB, PT = v["PA"], v["PB"], v["PT"]
    wtt_d, wpt_d, wgt_d = v["wtt_d"], v["wpt_d"], v["wgt_d"]
    wrt_d, wkt_d, wvt_d = v["wrt_d"], v["wkt_d"], v["wvt_d"]
    mur_d, muk_d = v["mur_d"], v["muk_d"]

    W = r.wait
    E = r.emit

    def wview(t, od):
        return t[:].rearrange("p (c o) -> p c o", c=NC)

    xnf3 = xnf[:].rearrange("p (c w) -> p c w", c=NC)
    t23 = [t2[0][:].rearrange("p (c w) -> p c w", c=NC)] * 4
    xr3 = xr_f[:].rearrange("p (c w) -> p c w", c=NC)
    xk3 = xk_f[:].rearrange("p (c w) -> p c w", c=NC)
    ka3 = k_act[:].rearrange("p (d w) -> p d w", d=NC)

    r.label = "prologue"
    for src, dst, od in ((wtt_d, w_th, L), (wpt_d, w_ph, L), (wgt_d, w_g, C),
                         (wrt_d, w_r, C), (wkt_d, w_k, C), (wvt_d, w_v, C)):
        E("gpsimd", lambda e, s=src, d=dst, o=od: e.dma_start(
            out=wview(d, o), in_=s.rearrange("(c p) o -> p c o", p=128)),
          "wl", 16, dma=True)
    E("sync", lambda e: e.dma_start(out=mur_s[:], in_=mur_d), "ml", 16)
    E("sync", lambda e: e.dma_start(out=muk_s[:], in_=muk_d), "ml", 16)
    wl_all = r.cnt["wl"]
    ml_all = r.cnt["ml"]

    E("gpsimd", lambda e: e.memset(ident[:], 0.0))
    E("gpsimd", lambda e: e.affine_select(
        out=ident[:], in_=ident[:], compare_op=ALU.not_equal, fill=1.0,
        base=0, pattern=[[-1, 128]], channel_multiplier=1), "gp")
    E("gpsimd", lambda e: e.memset(ones11[:], 1.0), "gp")
    gp_init = r.cnt["gp"]

    W("vector", sems["wl"], wl_all)
    W("vector", sems["ml"], ml_all)
    E("vector", lambda e: e.tensor_scalar(murm1[:], mur_s[:], 1.0, None,
                                          ALU.subtract), "dve")
    E("vector", lambda e: e.tensor_scalar(mukm1[:], muk_s[:], 1.0, None,
                                          ALU.subtract), "dve")
    d_mu = r.cnt["dve"]
    E("vector", lambda e: e.memset(k_rel[0][:, 0:128], 0.0), "dve")
    d_z = r.cnt["dve"]
    W("scalar", sems["dve"], d_z)
    for cc in range(NC):
        E("scalar", lambda e, cc=cc: e.activation(
            out=murm1_b[:, cc * 128:(cc + 1) * 128], in_=k_rel[0][:, 0:128],
            func=AF.Identity, bias=murm1[:, cc:cc + 1]), "act")
        E("scalar", lambda e, cc=cc: e.activation(
            out=mukm1_b[:, cc * 128:(cc + 1) * 128], in_=k_rel[0][:, 0:128],
            func=AF.Identity, bias=mukm1[:, cc:cc + 1]), "act")
    a_mub = r.cnt["act"]

    E("sync", lambda e: e.dma_start(out=x_t[0][0:1, :], in_=x_d[0:1, :]),
      "xl0", 16)
    W("vector", sems["xl0"], 16)
    E("vector", lambda e: e.reduce_sum(ssum[0:1, 0:1], x_t[0][0:1, :], axis=AX.X), "dve")
    E("vector", lambda e: e.tensor_mul(xn_tm[3][0:1, :], x_t[0][0:1, :], x_t[0][0:1, :]), "dve")
    E("vector", lambda e: e.reduce_sum(ssq[0:1, 0:1], xn_tm[3][0:1, 0:1024], axis=AX.X), "dve")
    E("vector", lambda e: e.tensor_scalar(mean[0:1, 0:1], ssum[0:1, 0:1], 1.0 / C, None, ALU.mult), "dve")
    E("vector", lambda e: e.tensor_mul(vtmp[0:1, 1:2], mean[0:1, 0:1], mean[0:1, 0:1]), "dve")
    E("vector", lambda e: e.tensor_scalar(vtmp[0:1, 0:1], ssq[0:1, 0:1], 1.0 / C, EPS, ALU.mult, ALU.add), "dve")
    E("vector", lambda e: e.tensor_sub(vtmp[0:1, 0:1], vtmp[0:1, 0:1], vtmp[0:1, 1:2]), "dve")
    d_c1 = r.cnt["dve"]
    W("scalar", sems["dve"], d_c1)
    E("scalar", lambda e: e.activation(out=vtmp[0:1, 1:2], in_=vtmp[0:1, 0:1], func=AF.Sqrt), "act")
    a_c1 = r.cnt["act"]
    W("vector", sems["act"], a_c1)
    E("vector", lambda e: e.reciprocal(rstd[0:1, 0:1], vtmp[0:1, 1:2]), "dve")
    E("vector", lambda e: e.scalar_tensor_tensor(nmr[0:1, 0:1], mean[0:1, 0:1], -1.0, rstd[0:1, 0:1], ALU.mult, ALU.mult), "dve")
    d_c2 = r.cnt["dve"]
    W("scalar", sems["dve"], d_c2)
    E("scalar", lambda e: e.activation(out=xn_tm[3][0:1, :], in_=x_t[0][0:1, :], func=AF.Identity,
                                       scale=rstd[0:1, 0:1], bias=nmr[0:1, 0:1]), "act")
    a_c2 = r.cnt["act"]
    W("tensor", sems["act"], a_c2)
    W("tensor", sems["gp"], gp_init)
    for cc in range(NC):
        E("tensor", lambda e, cc=cc: e.matmul(
            PA[:, cc:cc + 1], xn_tm[3][0:1, cc * 128:(cc + 1) * 128],
            ones11[0:1, 0:1], start=True, stop=True),
          "pe", 1 if cc == NC - 1 else None)
    pe_carry = r.cnt["pe"]
    W("vector", sems["pe"], pe_carry)
    E("vector", lambda e: e.tensor_copy(carry[:, 0:NC], PA[:, 0:NC]), "dve")
    d_carry0 = r.cnt["dve"]

    xl_cnt = [16]
    xn_done = {}
    xc_done = {}
    tr_done = {}
    ev_done = {}
    ldx_done = {}

    def emit_xload(g):
        if g >= NG:
            return
        if g - 2 >= 0:
            W("sync", sems["act"], xc_done[g - 2])
        elif g == 0:
            W("sync", sems["act"], a_c2)
        sname = f"xl{g % 2}"
        E("sync", lambda e, g=g: e.dma_start(
            out=x_t[g % 2][:], in_=x_d[1 + g * 128: 1 + (g + 1) * 128, :]),
          sname, 16)
        ldx_done[g] = (sname, r.cnt[sname])

    emit_xload(0)
    emit_xload(1)

    theta_ev = {}
    phi_ev = {}
    gx_ev = {}
    mix_done = {}
    rmm_done = {}
    rev_done = {}
    kdd_done = {}
    krel_done = {}
    ksq_done = {}
    vmm_done = {}
    vev_done = {}
    ffn_done = {}
    gate_done = {}
    spill_done = {}
    pmm_done = {}
    pdr_done = {}
    th_mm_done = {}
    pg_mm_done = {}

    gs_cnt = [0]

    a_stats_d = {}
    xr_done = {}
    xk_done = {}
    t_done = {}

    def emit_ln_chunk(st, j):
        par = st % 2
        sc = par * NJ + j
        g = st * NJ + j
        xb = x_t[g % 2]
        W("scalar", sems[ldx_done[g][0]], ldx_done[g][1])
        if g == 3:
            W("scalar", sems["pe"], pe_carry)
        if g - 4 >= 0:
            W("scalar", sems["pe"], tr_done[g - 4])
        E("scalar", lambda e, xb=xb, g=g, sc=sc: e.activation(
            out=xn_tm[g % 4][:], in_=xb[:], func=AF.Square,
            accum_out=ssq[:, sc:sc + 1]), "act")
        W("vector", sems[ldx_done[g][0]], ldx_done[g][1])
        E("vector", lambda e, xb=xb, sc=sc: e.reduce_sum(
            ssum[:, sc:sc + 1], xb[:], axis=AX.X), "dve", selfdep=False)
        E("vector", lambda e, sc=sc: e.tensor_scalar(
            nmr[:, sc:sc + 1], ssum[:, sc:sc + 1], -1.0 / C, None,
            ALU.mult), "dve")
        d_nm = r.cnt["dve"]
        W("scalar", sems["dve"], d_nm)
        E("scalar", lambda e, xb=xb, g=g, sc=sc: e.activation(
            out=xn_tm[g % 4][:], in_=xb[:], func=AF.Identity,
            bias=nmr[:, sc:sc + 1]), "act")
        xc_done[g] = r.cnt["act"]
        emit_xload(g + 2)
        if j == NJ - 1:
            a_stats_d[st] = r.cnt["act"]

    def emit_rstd_scale(st):
        par = st % 2
        sl = slice(par * NJ, par * NJ + NJ)
        W("vector", sems["act"], a_stats_d[st])
        E("vector", lambda e, sl=sl: e.tensor_scalar(
            vtmp[:, sl], ssq[:, sl], 1.0 / C, EPS, ALU.mult, ALU.add), "dve",
          selfdep=False)
        E("vector", lambda e, sl=sl: e.scalar_tensor_tensor(
            rstd[:, sl], nmr[:, sl], -1.0, nmr[:, sl], ALU.mult, ALU.mult),
          "dve")
        E("vector", lambda e, sl=sl: e.tensor_add(
            vtmp[:, sl], vtmp[:, sl], rstd[:, sl]), "dve")
        d_var = r.cnt["dve"]
        W("scalar", sems["dve"], d_var)
        E("scalar", lambda e, sl=sl: e.activation(
            out=rstd[:, sl], in_=vtmp[:, sl], func=AF.Sqrt), "act",
          selfdep=False)
        a_sqrt = r.cnt["act"]
        W("vector", sems["act"], a_sqrt)
        E("vector", lambda e, sl=sl: e.reciprocal(rstd[:, sl], rstd[:, sl]),
          "dve")
        for j in range(NJ):
            g = st * NJ + j
            sc = par * NJ + j
            W("vector", sems["act"], xc_done[g])
            E("vector", lambda e, g=g, sc=sc: e.tensor_scalar(
                xn_tm[g % 4][:], xn_tm[g % 4][:], rstd[:, sc:sc + 1], None,
                ALU.mult), "dve")
            xn_done[g] = r.cnt["dve"]

    def emit_T_chunk(st, j):
        g = st * NJ + j
        W("tensor", sems["dve"], xn_done[g])
        if g - 2 >= 0:
            W("tensor", sems["dve"], ev_done[g - 2])
        elif g == 0:
            W("tensor", sems["dve"], d_carry0)
        for cc in range(NC):
            E("tensor", lambda e, g=g, cc=cc: e.transpose(
                PT[g % 2][:, cc * 128:(cc + 1) * 128],
                xn_tm[g % 4][:, cc * 128:(cc + 1) * 128], ident[:]),
              "pe", 1 if cc == NC - 1 else None)
        tr_done[g] = r.cnt["pe"]
        W("vector", sems["pe"], tr_done[g])
        if st > 0 and j == 0:
            W("vector", sems["pe"], pg_mm_done[(st - 1, NJ - 1)])
        E("vector", lambda e, g=g, j=j: e.tensor_copy(
            xnf3[:, :, j * 128:(j + 1) * 128],
            PT[g % 2][:].rearrange("p (c t) -> p c t", c=NC)), "dve",
          selfdep=False)
        ev_done[g] = r.cnt["dve"]

    def emit_mix_chunk(st, j):
        g = st * NJ + j
        tb = t23[j % 4]
        mb = murm1_b[:].rearrange("p (c w) -> p c w", c=NC)
        kb = mukm1_b[:].rearrange("p (c w) -> p c w", c=NC)
        if st == 0 and j == 0:
            W("vector", sems["act"], a_mub)
            W("vector", sems["wl"], wl_all)
            W("vector", sems["ml"], ml_all)
        lo = j * 128
        if j == 0:
            E("vector", lambda e, tb=tb: e.tensor_sub(
                tb[:, :, 0:1], xnf3[:, :, 0:1],
                carry[:].rearrange("p (c w) -> p c w", w=1)), "dve")
            E("vector", lambda e, tb=tb: e.tensor_sub(
                tb[:, :, 1:128], xnf3[:, :, 1:128], xnf3[:, :, 0:127]), "dve")
        else:
            E("vector", lambda e, tb=tb, lo=lo: e.tensor_sub(
                tb[:, :, 0:128], xnf3[:, :, lo:lo + 128],
                xnf3[:, :, lo - 1:lo + 127]), "dve")
        if j == NJ - 1:
            E("vector", lambda e: e.tensor_copy(
                carry[:].rearrange("p (c w) -> p c w", w=1),
                xnf3[:, :, STW - 1:STW]), "dve")
        E("vector", lambda e, tb=tb, mb=mb, lo=lo: e.tensor_mul(
            xr3[:, :, lo:lo + 128], tb[:, :, 0:128], mb), "dve")
        E("vector", lambda e, lo=lo: e.tensor_add(
            xr3[:, :, lo:lo + 128], xr3[:, :, lo:lo + 128],
            xnf3[:, :, lo:lo + 128]), "dve")
        xr_done[(st, j)] = r.cnt["dve"]
        E("vector", lambda e, tb=tb, kb=kb, lo=lo: e.tensor_mul(
            xk3[:, :, lo:lo + 128], tb[:, :, 0:128], kb), "dve")
        E("vector", lambda e, lo=lo: e.tensor_add(
            xk3[:, :, lo:lo + 128], xk3[:, :, lo:lo + 128],
            xnf3[:, :, lo:lo + 128]), "dve")
        xk_done[(st, j)] = r.cnt["dve"]

    for j in range(NJ):
        emit_ln_chunk(0, j)
    emit_rstd_scale(0)
    for j in range(NJ):
        emit_T_chunk(0, j)
    for j in range(NJ):
        emit_mix_chunk(0, j)

    pacc4 = pacc[:].rearrange("p (lc hh o) -> p lc hh o", lc=2, hh=2)
    ppA_in4 = ppA_in.ap().rearrange("(lc p) (hh o) -> p lc hh o", p=128, hh=2)
    hA0 = ppA_out.ap()[0:L, :].rearrange("(lc p) (hh o) -> p lc hh o", p=128, hh=2)
    hA1 = ppA_out.ap()[L:2 * L, :].rearrange("(lc p) (hh o) -> p lc hh o", p=128, hh=2)
    pbA_done = [0]
    pb_pbf_v = [0]
    d_pacc_free_v = [None]

    def emit_exchange_A():
        W("gpsimd", sems["dve"], pdr_done[NST - 2])
        E("gpsimd", lambda e: e.dma_start(out=ppA_in4, in_=pacc4), "pbA", 16, dma=True)
        W("gpsimd", sems["pbA"], r.cnt["pbA"])
        E("gpsimd", lambda e: e.collective_compute(
            "AllGather", ALU.bypass,
            replica_groups=[[0, 1], [2, 3], [4, 5], [6, 7]],
            ins=[ppA_in.ap()], outs=[ppA_out.ap()]), "cc", 1, dma=True)
        W("gpsimd", sems["cc"], r.cnt["cc"])
        E("gpsimd", lambda e: e.dma_start(out=pacc4, in_=hA0), "pbA", 16, dma=True)
        W("gpsimd", sems["pbA"], r.cnt["pbA"])
        E("gpsimd", lambda e: e.dma_start(out=pacc4, in_=hA1, accum_op=ALU.add), "pbA", 16, dma=True)
        pbA_done[0] = r.cnt["pbA"]

    for st in range(NST):
        r.label = f"st{st}"
        par = st % 2
        if NST >= 2 and st == NST - 1:
            emit_exchange_A()

        W("tensor", sems["wl"], wl_all)
        if st > 0:
            W("tensor", sems["dve"], pdr_done[st - 1])
        W("tensor", sems["dve"], ev_done[st * NJ + NJ - 1])
        for lc in range(NL):
            for cc in range(NC):
                E("tensor", lambda e, lc=lc, cc=cc: e.matmul(
                    PA[:, lc * 512:(lc + 1) * 512],
                    w_th[:, cc * L + lc * 128: cc * L + (lc + 1) * 128],
                    xnf3[:, cc, :], start=(cc == 0), stop=(cc == NC - 1)),
                  "pe", 1 if (lc == NL - 1 and cc == NC - 1) else None)
        th_mm_done[st] = r.cnt["pe"]
        W("scalar", sems["pe"], th_mm_done[st])
        E("scalar", lambda e, st=st: e.activation(
            out=theta_f[:].rearrange("p (l T) -> p l T", l=NL)[:, :, st * STW:(st + 1) * STW],
            in_=PA[:, 0:1024].rearrange("p (l t) -> p l t", l=NL),
            func=AF.Copy), "act", selfdep=False)
        theta_ev[st] = r.cnt["act"]

        def emit_r_group(j, theta_wait, extra_dve=None):
            W("tensor", sems["dve"], xr_done[(st, j)])
            if extra_dve:
                W("tensor", sems["dve"], extra_dve)
            if theta_wait is not None:
                W("tensor", sems["act"], theta_wait)
            elif j > 0:
                W("tensor", sems["act"], rev_done[(st, j - 1)])
            for cc in range(NC):
                for hh in range(2):
                    E("tensor", lambda e, j=j, cc=cc, hh=hh: e.matmul(
                        PA[:, hh * 512:(hh + 1) * 512],
                        xr3[:, cc, j * 128:(j + 1) * 128],
                        w_r[:, cc * C + hh * 512: cc * C + (hh + 1) * 512],
                        start=(cc == 0), stop=(cc == NC - 1)),
                      "pe", 1 if (cc == NC - 1 and hh == 1) else None)
            rmm_done[(st, j)] = r.cnt["pe"]
            W("scalar", sems["pe"], rmm_done[(st, j)])
            if st > 0:
                fd = ffn_done[(st - 1, j)]
                W("scalar", sems[fd[0]], fd[1])
            E("scalar", lambda e, j=j: e.activation(
                out=r_tm[:, j * C:(j + 1) * C], in_=PA[:, 0:1024],
                func=AF.Sigmoid), "act", selfdep=False)
            rev_done[(st, j)] = r.cnt["act"]

        def emit_k_group(dd, extra_dve=None):
            reg = PA[:, 1024:1536] if dd % 2 == 0 else PB[:, 1024:1536]
            W("tensor", sems["dve"], xk_done[(st, NJ - 1)])
            if extra_dve:
                W("tensor", sems["dve"], extra_dve)
            if dd == 0:
                W("tensor", sems["act"], phi_ev[(st, NJ - 1)])
            elif dd >= 2:
                W("tensor", sems["dve"], krel_done[(st, dd - 2)])
            for cc in range(NC):
                E("tensor", lambda e, reg=reg, dd=dd, cc=cc: e.matmul(
                    reg, w_k[:, cc * C + dd * 128: cc * C + (dd + 1) * 128],
                    xk3[:, cc, :], start=(cc == 0), stop=(cc == NC - 1)),
                  "pe", 1 if cc == NC - 1 else None)
            kdd_done[(st, dd)] = r.cnt["pe"]
            W("vector", sems["pe"], kdd_done[(st, dd)])
            E("vector", lambda e, reg=reg, dd=dd: e.tensor_scalar(
                k_rel[dd % 2][:], reg, 0.0, None, ALU.max), "dve")
            krel_done[(st, dd)] = r.cnt["dve"]
            E("vector", lambda e, dd=dd: e.tensor_mul(
                ka3[:, dd, :], k_rel[dd % 2][:], k_rel[dd % 2][:]), "dve")
            ksq_done[(st, dd)] = r.cnt["dve"]

        last = (NST >= 2 and st == NST - 1)
        d_kb = [0]

        for j in range(NJ):
            if st > 0 and j == 0:
                W("tensor", sems["dve"], pdr_done[st - 1])
                W("tensor", sems["act"], vev_done.get((st - 1, NJ - 1), 0))
            if j > 0:
                W("tensor", sems["act"], phi_ev[(st, j - 1)])
                W("tensor", sems["act"], gx_ev[(st, j - 1)])
            for cc in range(NC):
                E("tensor", lambda e, j=j, cc=cc: e.matmul(
                    PA[:, 1024:1280],
                    xnf3[:, cc, j * 128:(j + 1) * 128],
                    w_ph[:, cc * L:(cc + 1) * L],
                    start=(cc == 0), stop=(cc == NC - 1)))
                for hh in range(2):
                    E("tensor", lambda e, j=j, cc=cc, hh=hh: e.matmul(
                        PB[:, hh * 512:(hh + 1) * 512],
                        xnf3[:, cc, j * 128:(j + 1) * 128],
                        w_g[:, cc * C + hh * 512: cc * C + (hh + 1) * 512],
                        start=(cc == 0), stop=(cc == NC - 1)),
                      "pe", 1 if (cc == NC - 1 and hh == 1) else None)
            pg_mm_done[(st, j)] = r.cnt["pe"]
            W("scalar", sems["pe"], pg_mm_done[(st, j)])
            E("scalar", lambda e, j=j: e.activation(
                out=phi_tm[:, j * L:(j + 1) * L], in_=PA[:, 1024:1280],
                func=AF.Copy), "act", selfdep=False)
            phi_ev[(st, j)] = r.cnt["act"]
            E("scalar", lambda e, j=j: e.activation(
                out=gx_tm[:, j * C:(j + 1) * C], in_=PB[:, 0:1024],
                func=AF.Copy), "act", selfdep=False)
            gx_ev[(st, j)] = r.cnt["act"]

            if not last:
                emit_r_group(j, theta_ev[st] if j == 0 else None)

            if st + 1 < NST:
                emit_ln_chunk(st + 1, j)

        if last:
            W("tensor", sems["act"], theta_ev[st])
            W("tensor", sems["act"], phi_ev[(st, NJ - 1)])
            W("tensor", sems["act"], gx_ev[(st, NJ - 1)])
            pregs0 = (PA[:, 0:512], PA[:, 512:1024], PA[:, 1024:1536], PB[:, 0:512])
            for j in range(NJ):
                for q, reg in enumerate(pregs0):
                    lc, hh = q // 2, q % 2
                    E("tensor", lambda e, reg=reg, j=j, lc=lc, hh=hh: e.matmul(
                        reg, phi_tm[:, j * L + lc * 128: j * L + (lc + 1) * 128],
                        gx_tm[:, j * C + hh * 512: j * C + (hh + 1) * 512],
                        start=(j == 0), stop=(j == NJ - 1)),
                      "pe", 1 if (j == NJ - 1 and q == 3) else None)
            pmm_done[st] = r.cnt["pe"]
            W("vector", sems["pe"], pmm_done[st])
            E("vector", lambda e: e.tensor_scalar(
                kbex[:, 0:1536], PA[:, 0:1536], SCALE, None, ALU.mult), "dve")
            E("vector", lambda e: e.tensor_scalar(
                kbex[:, 1536:2048], PB[:, 0:512], SCALE, None, ALU.mult), "dve")
            d_kb[0] = r.cnt["dve"]
            kbex4 = kbex[:].rearrange("p (lc hh o) -> p lc hh o", lc=2, hh=2)
            pp_in4_ = pp_in.ap().rearrange("(lc p) (hh o) -> p lc hh o", p=128, hh=2)
            W("gpsimd", sems["dve"], d_kb[0])
            E("gpsimd", lambda e: e.dma_start(out=pp_in4_, in_=kbex4), "pb", 16, dma=True)
            W("gpsimd", sems["pb"], r.cnt["pb"])
            E("gpsimd", lambda e: e.collective_compute(
                "AllGather", ALU.bypass,
                replica_groups=[[0, 1], [2, 3], [4, 5], [6, 7]],
                ins=[pp_in.ap()], outs=[pp_out.ap()]), "cc", 1, dma=True)
            cc_b_last = r.cnt["cc"]
            pbf4_l = p_bf[:].rearrange("p (lc hh o) -> p lc hh o", lc=2, hh=2)
            hB0 = pp_out.ap()[0:L, :].rearrange("(lc p) (hh o) -> p lc hh o", p=128, hh=2)
            hB1 = pp_out.ap()[L:2 * L, :].rearrange("(lc p) (hh o) -> p lc hh o", p=128, hh=2)
            W("gpsimd", sems["pbA"], pbA_done[0])
            E("gpsimd", lambda e: e.dma_start(out=p_bf[:], in_=pacc[:]),
              "pb", 16, dma=True)
            d_pacc_free_v[0] = ("pb", r.cnt["pb"])
            W("gpsimd", sems["pb"], r.cnt["pb"])
            W("gpsimd", sems["cc"], cc_b_last)
            E("gpsimd", lambda e, hB0=hB0: e.dma_start(
                out=pbf4_l, in_=hB0, accum_op=ALU.add), "pb", 16, dma=True)
            W("gpsimd", sems["pb"], r.cnt["pb"])
            E("gpsimd", lambda e, hB1=hB1: e.dma_start(
                out=pbf4_l, in_=hB1, accum_op=ALU.add), "pb", 16, dma=True)
            pb_pbf_v[0] = r.cnt["pb"]
            for j in range(NJ):
                emit_r_group(j, None, extra_dve=d_kb[0])

        for dd in range(NC):
            emit_k_group(dd, extra_dve=(d_kb[0] if (last and dd == 0) else None))
        if st + 1 < NST:
            emit_rstd_scale(st + 1)

        pregs = (PA[:, 0:512], PA[:, 512:1024], PA[:, 1024:1536], PB[:, 0:512])
        for j in range(NJ):
            g = st * NJ + j
            if not last:
                if j == 0:
                    W("tensor", sems["act"], rev_done[(st, NJ - 1)])
                    W("tensor", sems["dve"], krel_done[(st, NC - 2)])
                    W("tensor", sems["act"], gx_ev[(st, NJ - 1)])
                for q, reg in enumerate(pregs):
                    lc, hh = q // 2, q % 2
                    E("tensor", lambda e, reg=reg, j=j, lc=lc, hh=hh: e.matmul(
                        reg, phi_tm[:, j * L + lc * 128: j * L + (lc + 1) * 128],
                        gx_tm[:, j * C + hh * 512: j * C + (hh + 1) * 512],
                        start=(j == 0), stop=(j == NJ - 1)),
                      "pe", 1 if (j == NJ - 1 and q == 3) else None)
            if j == NJ - 1 and not last:
                pmm_done[st] = r.cnt["pe"]
                if not (NST >= 2 and st == NST - 1):
                    W("vector", sems["pe"], pmm_done[st])
                    if st == 0:
                        E("vector", lambda e: e.tensor_scalar(
                            pacc[:, 0:1536], PA[:, 0:1536], SCALE, None, ALU.mult), "dve")
                        E("vector", lambda e: e.tensor_scalar(
                            pacc[:, 1536:2048], PB[:, 0:512], SCALE, None, ALU.mult), "dve")
                    else:
                        E("vector", lambda e: e.scalar_tensor_tensor(
                            pacc[:, 0:1536], PA[:, 0:1536], SCALE, pacc[:, 0:1536],
                            ALU.mult, ALU.add), "dve")
                        E("vector", lambda e: e.scalar_tensor_tensor(
                            pacc[:, 1536:2048], PB[:, 0:512], SCALE, pacc[:, 1536:2048],
                            ALU.mult, ALU.add), "dve")
                    pdr_done[st] = r.cnt["dve"]

            if st + 1 < NST:
                emit_T_chunk(st + 1, j)
                emit_mix_chunk(st + 1, j)

            if j == 0:
                W("tensor", sems["dve"], max(ksq_done[(st, dd)] for dd in range(NC)))
                W("tensor", sems["act"], gx_ev[(st, NJ - 1)])
            else:
                W("tensor", sems["act"], vev_done[(st, j - 1)])
            for dd in range(NC):
                for hh in range(2):
                    E("tensor", lambda e, j=j, dd=dd, hh=hh: e.matmul(
                        PB[:, 512 + hh * 512: 512 + (hh + 1) * 512],
                        ka3[:, dd, j * 128:(j + 1) * 128],
                        w_v[:, dd * C + hh * 512: dd * C + (hh + 1) * 512],
                        start=(dd == 0), stop=(dd == NC - 1)),
                      "pe", 1 if (dd == NC - 1 and hh == 1) else None)
            vmm_done[(st, j)] = r.cnt["pe"]
            W("scalar", sems["pe"], vmm_done[(st, j)])
            if g - 2 >= 0:
                gd = gate_done[(st, j - 2)] if j - 2 >= 0 else gate_done[(st - 1, j + 2)]
                W("scalar", sems[gd[0]], gd[1])
            E("scalar", lambda e, g=g: e.activation(
                out=v_tm[g % 2][:], in_=PB[:, 512:1536], func=AF.Copy),
              "act", selfdep=False)
            vev_done[(st, j)] = r.cnt["act"]
            feng, fsem = ("vector", "dve") if last else ("gpsimd", "gp")
            W(feng, sems["act"], vev_done[(st, j)])
            W(feng, sems["act"], rev_done[(st, j)])
            E(feng, lambda e, g=g, j=j: e.tensor_mul(
                v_tm[g % 2][:], r_tm[:, j * C:(j + 1) * C], v_tm[g % 2][:]),
              fsem, selfdep=False)
            ffn_done[(st, j)] = (fsem, r.cnt[fsem])
            W("scalar", sems[fsem], ffn_done[(st, j)][1])
            if g - 2 >= 0:
                W("scalar", sems[spill_done[g - 2][0]], spill_done[g - 2][1])
            E("scalar", lambda e, g=g: e.activation(
                out=sig_tm[g % 2][:], in_=v_tm[g % 2][:], func=AF.Sigmoid),
              "act", selfdep=False)
            a_sig = r.cnt["act"]
            W(feng, sems["act"], a_sig)
            E(feng, lambda e, g=g: e.tensor_mul(
                sig_tm[g % 2][:], v_tm[g % 2][:], sig_tm[g % 2][:]), fsem)
            gate_done[(st, j)] = (fsem, r.cnt[fsem])
            W("sync", sems[gate_done[(st, j)][0]], gate_done[(st, j)][1])
            sname = f"gs{g % 2}"
            if g - 2 >= 0:
                W("sync", sems[sname], r.cnt[sname])
            E("sync", lambda e, g=g: e.dma_start(
                out=gate_d[g * 128:(g + 1) * 128, :], in_=sig_tm[g % 2][:]),
              sname, 16)
            spill_done[g] = (sname, r.cnt[sname])


    r.label = "exchange"
    ksend = k_act[:, 0:2048]
    pp_in4 = pp_in.ap().rearrange("(lc p) (hh o) -> p lc hh o", p=128, hh=2)
    ksend4 = ksend.rearrange("p (lc hh o) -> p lc hh o", lc=2, hh=2)
    pbf4 = p_bf[:].rearrange("p (lc hh o) -> p lc hh o", lc=2, hh=2)
    h0 = pp_out.ap()[0:L, :].rearrange("(lc p) (hh o) -> p lc hh o", p=128, hh=2)
    h1 = pp_out.ap()[L:2 * L, :].rearrange("(lc p) (hh o) -> p lc hh o", p=128, hh=2)
    if NST >= 2:
        pb_pbf = pb_pbf_v[0]
        d_pacc_free = d_pacc_free_v[0]
    else:
        W("vector", sems["pe"], vmm_done[(NST - 1, NJ - 1)])
        E("vector", lambda e: e.tensor_copy(ksend, pacc[:]), "dve")
        d_scaled = r.cnt["dve"]
        d_pacc_free = ("dve", d_scaled)
        W("gpsimd", sems["dve"], d_scaled)
        E("gpsimd", lambda e: e.dma_start(out=pp_in4, in_=ksend4), "pb", 16, dma=True)
        W("gpsimd", sems["pb"], r.cnt["pb"])
        E("gpsimd", lambda e: e.collective_compute(
            "AllGather", ALU.bypass,
            replica_groups=[[0, 1], [2, 3], [4, 5], [6, 7]],
            ins=[pp_in.ap()], outs=[pp_out.ap()]), "cc", 1, dma=True)
        cc_b = r.cnt["cc"]
        W("gpsimd", sems["cc"], cc_b)
        E("gpsimd", lambda e: e.dma_start(out=pbf4, in_=h0), "pb", 16, dma=True)
        W("gpsimd", sems["pb"], r.cnt["pb"])
        E("gpsimd", lambda e: e.dma_start(out=pbf4, in_=h1, accum_op=ALU.add), "pb", 16, dma=True)
        pb_pbf = r.cnt["pb"]

    sgt = []
    for buf in (xnf, xr_f, xk_f, k_act):
        bc = buf[:].bitcast(F32)
        sgt.append(bc[:, 0:1024])
        sgt.append(bc[:, 1024:2048])
    g2 = [r_tm[:, i * 1024:(i + 1) * 1024] for i in range(4)]
    pacc_f = pacc[:]
    x2 = [x_t[0][:], x_t[1][:], pacc_f[:, 0:1024], pacc_f[:, 1024:2048]]
    g2_done = {}
    smm_done = {}
    op1_done = {}
    op2_done = {}
    x2_done = {}
    store_done = {}

    def emit_g2load(g, eng="scalar"):
        if g >= NG:
            return
        W(eng, sems[spill_done[g][0]], spill_done[g][1])
        if g <= 3:
            fd = ffn_done[(NST - 1, g)]
            W(eng, sems[fd[0]], fd[1])
        else:
            W(eng, sems["dve"], op1_done[g - 4])
        sname = f"g2_{g % 4}"
        E(eng, lambda e, g=g: e.dma_start(
            out=g2[g % 4], in_=gate_d[g * 128:(g + 1) * 128, :]),
          sname, 16, dma=True)
        g2_done[g] = (sname, r.cnt[sname])

    def emit_x2load(g, eng="scalar"):
        if g >= NG:
            return
        if g == 0:
            W(eng, sems["pbA"], pbA_done[0])
            W(eng, sems[d_pacc_free[0]], d_pacc_free[1])
        if g - 4 >= 0:
            W(eng, sems[store_done[g - 4][0]], store_done[g - 4][1])
        sname = f"x2_{g % 4}"
        E(eng, lambda e, g=g: e.dma_start(
            out=x2[g % 4], in_=x_d[1 + g * 128: 1 + (g + 1) * 128, :]),
          sname, 16, dma=True)
        x2_done[g] = (sname, r.cnt[sname])

    for g in range(4):
        emit_g2load(g, "sync")
    for g in range(4):
        emit_x2load(g, "sync")

    p2regs = [(PA[:, 0:512], PA[:, 512:1024]),
              (PA[:, 1024:1536], PB[:, 0:512]),
              (PB[:, 512:1024], PB[:, 1024:1536])]
    for g in range(NG):
        r.label = f"p2g{g}"
        rpair = p2regs[g % 3]
        if g == 0:
            W("tensor", sems["pb"], pb_pbf)
            W("tensor", sems["act"], theta_ev[NST - 1])
            W("tensor", sems["act"], vev_done[(NST - 1, NJ - 1)])
            W("tensor", sems["dve"], max(ksq_done[(NST - 1, dd)] for dd in range(NC)))
        if g - 3 >= 0:
            W("tensor", sems["dve"], op1_done[g - 3])
        for hh in range(2):
            for lc in range(NL):
                E("tensor", lambda e, rg=rpair[hh], g=g, lc=lc, hh=hh: e.matmul(
                    rg,
                    theta_f[:, lc * TC + g * 128: lc * TC + (g + 1) * 128],
                    p_bf[:, (lc * 2 + hh) * 512:(lc * 2 + hh + 1) * 512],
                    start=(lc == 0), stop=(lc == NL - 1)),
                  "pe", 1 if (lc == NL - 1 and hh == 1) else None)
        smm_done[g] = r.cnt["pe"]
        W("vector", sems["pe"], smm_done[g])
        W("vector", sems[g2_done[g][0]], g2_done[g][1])
        if g - 8 >= 0:
            W("vector", sems[store_done[g - 8][0]], store_done[g - 8][1])
        for hh in range(2):
            E("vector", lambda e, rg=rpair[hh], g=g, hh=hh: e.tensor_mul(
                sgt[g % 8][:, hh * 512:(hh + 1) * 512], rg,
                g2[g % 4][:, hh * 512:(hh + 1) * 512]), "dve")
        op1_done[g] = r.cnt["dve"]
        emit_g2load(g + 4)
        eng2 = "vector" if g % 2 else "gpsimd"
        sem2 = "dve" if g % 2 else "gp"
        if eng2 == "gpsimd":
            W(eng2, sems["dve"], op1_done[g])
        W(eng2, sems[x2_done[g][0]], x2_done[g][1])
        E(eng2, lambda e, g=g: e.tensor_add(
            sgt[g % 8][:], sgt[g % 8][:], x2[g % 4]), sem2)
        op2_done[g] = (r.cnt[sem2], sem2)
        cntv, semn = op2_done[g]
        W("sync", sems[semn], cntv)
        s1 = f"st1_{g % 8}"
        if g - 8 >= 0:
            W("sync", sems[store_done[g - 8][0]], store_done[g - 8][1])
        E("sync", lambda e, g=g: e.dma_start(
            out=out_d[g * 128:(g + 1) * 128, :], in_=sgt[g % 8][:]), s1, 16)
        store_done[g] = (s1, r.cnt[s1])
        emit_x2load(g + 4)

    if v.get("dbg"):
        for i in range(8):
            W("gpsimd", sems[f"st1_{i}"], r.cnt.get(f"st1_{i}", 0))
        E("gpsimd", lambda e: e.dma_start(out=v["dbg_st"][:, 0:8], in_=ssum[:]), "pb", 16, dma=True)
        E("gpsimd", lambda e: e.dma_start(out=v["dbg_st"][:, 8:16], in_=ssq[:]), "pb", 16, dma=True)
        E("gpsimd", lambda e: e.dma_start(out=v["dbg_st"][:, 16:24], in_=vtmp[:]), "pb", 16, dma=True)
        E("gpsimd", lambda e: e.dma_start(out=v["dbg_st"][:, 24:32], in_=rstd[:]), "pb", 16, dma=True)
        E("gpsimd", lambda e: e.dma_start(out=v["dbg_st"][:, 32:40], in_=nmr[:]), "pb", 16, dma=True)
        E("gpsimd", lambda e: e.dma_start(out=v["dbg_th"], in_=theta_f[:]), "pb", 16, dma=True)
        E("gpsimd", lambda e: e.dma_start(out=v["dbg_pac"], in_=pacc[:]), "pb", 16, dma=True)
        E("gpsimd", lambda e: e.dma_start(out=v["dbg_gate"], in_=gate_d), "pb", 16, dma=True)
        E("gpsimd", lambda e: e.dma_start(out=v["dbg_xnf"], in_=xnf[:]), "pb", 16, dma=True)
        W("gpsimd", sems["pb"], r.cnt["pb"])

    for i in range(8):
        W("gpsimd", sems[f"st1_{i}"], r.cnt.get(f"st1_{i}", 0))

    with nc.Block() as block:
        for eng in ENGINES:
            def body(e, eng=eng):
                for f in r.ops[eng]:
                    f(e)
            getattr(block, eng)(body)


def kernel(L0_attn_x, L0_attn_W_theta, L0_attn_W_phi, L0_attn_W_g,
           L0_ffn_W_r, L0_ffn_W_k, L0_ffn_W_v, L0_ffn_mu_r, L0_ffn_mu_k,
           _dbg=False, _results=None):
    x = np.asarray(L0_attn_x, dtype=np.float32)
    B, T, C_ = x.shape
    half = T // 2
    wtt = np.ascontiguousarray(np.asarray(L0_attn_W_theta, dtype=np.float32).T)
    wpt = np.ascontiguousarray(np.asarray(L0_attn_W_phi, dtype=np.float32).T)
    wgt = np.ascontiguousarray(np.asarray(L0_attn_W_g, dtype=np.float32).T)
    wrt = np.ascontiguousarray(np.asarray(L0_ffn_W_r, dtype=np.float32).T)
    wkt = np.ascontiguousarray(np.asarray(L0_ffn_W_k, dtype=np.float32).T)
    wvt = np.ascontiguousarray(np.asarray(L0_ffn_W_v, dtype=np.float32).T)
    mur = np.ascontiguousarray(np.asarray(L0_ffn_mu_r, dtype=np.float32).reshape(NC, 128).T)
    muk = np.ascontiguousarray(np.asarray(L0_ffn_mu_k, dtype=np.float32).reshape(NC, 128).T)

    nc = build_program(half, dbg=_dbg)
    zrow = np.zeros((1, C_), np.float32)
    in_maps = []
    for c in range(8):
        b, h = c // 2, c % 2
        t0 = h * half
        prev = zrow if h == 0 else x[b, t0 - 1:t0]
        xs = np.ascontiguousarray(np.concatenate([prev, x[b, t0:t0 + half]], axis=0))
        in_maps.append({"x": xs, "wtt": wtt, "wpt": wpt, "wgt": wgt,
                        "wrt": wrt, "wkt": wkt, "wvt": wvt,
                        "mur": mur, "muk": muk})
    res = run_bass_kernel_spmd(nc, in_maps, core_ids=list(range(8)))
    if _results is not None:
        _results.append(res)
    out = np.empty((B, T, C_), np.float32)
    for c in range(8):
        b, h = c // 2, c % 2
        out[b, h * half:(h + 1) * half] = res.results[c]["out"]
    return out
